# version 11
# baseline (speedup 1.0000x reference)
import contextlib
import numpy as np
import ml_dtypes
import concourse.bass as bass
import concourse.mybir as mybir
from concourse.bass_utils import run_bass_kernel_spmd

F32 = mybir.dt.float32
BF16 = mybir.dt.bfloat16
AF = mybir.ActivationFunctionType
ALU = mybir.AluOpType
AX = mybir.AxisListType
NPBF = ml_dtypes.bfloat16

B, L, D = 2, 8192, 1024
TOK = 2048
NPT = TOK // 128
NF = D // 128
H, DH = 16, 64
KT = L // 128
GROUPS = [[0, 1, 2, 3], [4, 5, 6, 7]]
L0_SCALE = 1.0 / 8.0
L1_SCALE = 1.0 / 32.0


class Sem:
    __slots__ = ("h", "n")

    def __init__(self, h):
        self.h = h
        self.n = 0


class Prog:
    ENGINES = ("sync", "vector", "scalar", "tensor", "gpsimd")

    def __init__(self, nc, ctx):
        self.nc = nc
        self.ctx = ctx
        self.stream = {e: [] for e in self.ENGINES}

    def sem(self, name):
        s = Sem(self.ctx.enter_context(self.nc.semaphore(name)))
        if not hasattr(self, "all_sems"):
            self.all_sems = []
        self.all_sems.append(s)
        return s

    def phase_barrier(self):
        for e in self.ENGINES:
            for s in getattr(self, "all_sems", []):
                self.wait(e, s, s.n)

    def sems(self, name, k):
        return [self.sem(f"{name}{i}") for i in range(k)]

    def wait(self, eng, sem, val):
        if val > 0:
            self.stream[eng].append(lambda e, s=sem.h, v=val: e.wait_ge(s, v))

    def barrier(self, eng, *sems):
        for s in sems:
            self.wait(eng, s, s.n)

    def op(self, eng, opname, *args, wait=(), inc=(), **kw):
        for s, v in wait:
            self.wait(eng, s, v)
        inc = tuple(inc)

        def f(e, opname=opname, args=args, kw=kw, inc=inc):
            ins = getattr(e, opname)(*args, **kw)
            for s, k in inc:
                ins = ins.then_inc(s.h, k)
            return ins

        self.stream[eng].append(f)
        ret = None
        for s, k in inc:
            s.n += k
            if ret is None:
                ret = s.n
        return ret

    def dma(self, eng, sem, pairs, wait=()):
        for j, (o, i_) in enumerate(pairs):
            self.op(eng, "dma_start", out=o, in_=i_,
                    wait=(wait if j == 0 else ()), inc=[(sem, 16)])
        return (sem, sem.n)

    def build(self, block):
        for eng in self.ENGINES:
            fns = self.stream[eng]
            if not fns:
                continue
            getattr(block, eng)(lambda e, fns=fns: [f(e) for f in fns])


def emit_ident(p, ident, sg):
    v = p.op("gpsimd", "memset", ident[:], 0.0, inc=[(sg, 1)])
    return p.op("gpsimd", "affine_select", out=ident[:], in_=ident[:],
                compare_op=ALU.not_equal, fill=1.0, base=0,
                pattern=[[-1, 128]], channel_multiplier=1,
                wait=[(sg, v)], inc=[(sg, 1)])


def alloc_ln_stats(nc, ctx, n, scr=True, pre=""):
    st = {}
    for name in ("s1", "ssq", "mean", "msq", "var", "std", "rstd", "nmr"):
        st[name] = ctx.enter_context(nc.sbuf_tensor(pre + "ln_" + name, [128, n], F32))
    if scr:
        st["scr"] = ctx.enter_context(nc.sbuf_tensor(pre + "ln_scr", [128, D], F32))
    st["eps"] = ctx.enter_context(nc.sbuf_tensor(pre + "ln_eps", [128, 1], F32))
    return st


def emit_ln_stats_finish(p, st, n, sv, ss, sq_done=None, central=True):
    w = [sq_done] if sq_done else []
    if not central:
        p.barrier("vector", sv)
        v = p.op("vector", "tensor_mul", st["msq"][:, 0:n], st["mean"][:, 0:n],
                 st["mean"][:, 0:n], inc=[(sv, 1)])
        p.wait("vector", sv, v)
        v2 = p.op("vector", "scalar_tensor_tensor", st["var"][:, 0:n],
                  st["ssq"][:, 0:n], 1.0 / D, st["msq"][:, 0:n],
                  op0=ALU.mult, op1=ALU.subtract, wait=w, inc=[(sv, 1)])
        w = [(sv, v2)]
        src = st["var"]
        sc = 1.0
    else:
        src = st["ssq"]
        sc = 1.0 / D
    s_std = p.op("scalar", "activation", out=st["std"][:, 0:n], in_=src[:, 0:n],
                 func=AF.Sqrt, bias=st["eps"][:, 0:1], scale=sc,
                 wait=w, inc=[(ss, 1)])
    v = p.op("vector", "reciprocal", st["rstd"][:, 0:n], st["std"][:, 0:n],
             wait=[(ss, s_std)], inc=[(sv, 1)])
    p.wait("vector", sv, v)
    v6 = p.op("vector", "scalar_tensor_tensor", st["nmr"][:, 0:n], st["mean"][:, 0:n],
              -1.0, st["rstd"][:, 0:n], op0=ALU.mult, op1=ALU.mult, inc=[(sv, 1)])
    return v6


def emit_transpose8(p, dst3, col0, src, ident, ps_tr, sm, sv, wait_first=()):
    last = 0
    for half in range(2):
        pt = ps_tr[half]
        tm = None
        for j in range(4):
            f = half * 4 + j
            w = list(wait_first) if (half == 0 and j == 0) else []
            tm = p.op("tensor", "transpose", pt[:, 128 * j:128 * (j + 1)],
                      src[:, 128 * f:128 * (f + 1)], ident[:], wait=w,
                      inc=([(sm, 1)] if j == 3 else []))
        last = p.op("vector", "tensor_copy",
                    dst3[:, 4 * half:4 * half + 4, col0:col0 + 128],
                    pt[:, 0:512].rearrange("p (f q) -> p f q", f=4),
                    wait=[(sm, tm)], inc=[(sv, 1)])
    return last


L0PRE_IO = dict(x_own=([TOK + 1, D], F32), w_qkv=([D, 3 * D], F32),
                b_qkv=([1, 3 * D], F32), rope_cos=([TOK, 512], F32),
                rope_sin=([TOK, 512], F32))
L0PRE_OUT = dict(xnt=([D, TOK + 1], BF16), qt=([D, TOK], BF16),
                 kt=([D, TOK], BF16), v=([TOK, D], BF16))


def emit_l0pre(nc, p, ctx, io, pre="", hook=None):
    x_own, w_qkv, b_qkv = io["x_own"], io["w_qkv"], io["b_qkv"]
    rope_cos, rope_sin = io["rope_cos"], io["rope_sin"]
    xnt_o, qt_o, kt_o, v_o = io["xnt"], io["qt"], io["kt"], io["v"]
    if True:
        sb = lambda name, shape, dt=F32: ctx.enter_context(nc.sbuf_tensor(pre + name, shape, dt))
        psa = lambda name, shape, dt=F32: ctx.enter_context(nc.psum_tensor(pre + name, shape, dt))
        x_s = [sb(f"x{i}", [128, D]) for i in range(3)]
        xb_s = sb("xb", [1, D])
        xn_s = [sb(f"xn{i}", [128, D], BF16) for i in range(2)]
        xnb_s = sb("xnb", [1, D], BF16)
        xnt = sb("xnt_s", [128, NF, TOK + 1], BF16)
        qT = sb("qT_s", [128, NF, TOK], BF16)
        kT = sb("kT_s", [128, NF, TOK], BF16)
        wst = [sb(f"wst{i}", [128, D]) for i in range(2)]
        wph = sb("wph", [128, NF, D], BF16)
        bias_f = sb("bias_f", [1, 3 * D])
        bias_h = sb("bias_h", [1, 3 * D], BF16)
        bias_l = sb("bias_l", [1, 3 * D], BF16)
        one_r = sb("one_r", [1, 128], BF16)
        qf = sb("qf", [128, D])
        qr = [sb(f"qr{i}", [128, D], BF16) for i in range(2)]
        vt_b = sb("vt0", [128, D], BF16)
        cosb = sb("cosb0", [128, 512])
        sinb = sb("sinb0", [128, 512])
        t1m, t2m = sb("t1m", [128, 512]), sb("t2m", [128, 512])
        t3m, t4m = sb("t3m", [128, 512]), sb("t4m", [128, 512])
        ident_bf = sb("ident_bf", [128, 128], BF16)
        st = alloc_ln_stats(nc, ctx, NPT + 1, scr=False, pre=pre)
        st["scr"] = qf
        ps_mm = [psa(f"psmm{i}", [128, 512]) for i in range(3)]
        ps_tr = [psa(f"pstr{i}", [128, 1024], BF16) for i in range(2)]

        sv, ss, sg, sm = (p.sem(pre + n) for n in ("sv", "ss", "sg", "sm"))
        lx = p.sems(pre + "lx", 3)
        lmisc = p.sem(pre + "lmisc")
        lw = p.sems(pre + "lw", 2)
        lcs = p.sem(pre + "lcs")
        s_v = p.sem(pre + "s_v")
        s_fin = p.sem(pre + "s_fin")
        g_id = emit_ident(p, ident_bf, sg)

        w_xb = p.dma("sync", lmisc, [(xb_s[:], x_own[0:1, :]), (bias_f[:], b_qkv)])
        p.op("vector", "memset", st["s1"][:], 0.0)
        p.op("vector", "memset", st["ssq"][:], 0.0)
        p.op("vector", "memset", one_r[:], 1.0)
        v0 = p.op("vector", "memset", st["eps"][:], 1e-5, inc=[(sv, 1)])
        p.wait("vector", sv, v0)
        v = p.op("vector", "tensor_copy", bias_h[:], bias_f[:], wait=[w_xb],
                 inc=[(sv, 1)])
        v_bias = 0
        for ch in range(3):
            p.wait("vector", sv, v if ch == 0 else v_bias)
            v2 = p.op("vector", "tensor_sub", qf[0:1, :],
                      bias_f[0:1, D * ch:D * (ch + 1)],
                      bias_h[0:1, D * ch:D * (ch + 1)], inc=[(sv, 1)])
            p.wait("vector", sv, v2)
            v_bias = p.op("vector", "tensor_copy", bias_l[0:1, D * ch:D * (ch + 1)],
                          qf[0:1, :], inc=[(sv, 1)])

        red_v = [0] * NPT
        sq_v = [0] * NPT
        for i in range(NPT):
            w = [(sv, red_v[i - 3]), (ss, sq_v[i - 3])] if i >= 3 else []
            wl = p.dma("sync", lx[i % 3],
                       [(x_s[i % 3][:], x_own[1 + 128 * i: 1 + 128 * (i + 1), :])],
                       wait=w)
            red_v[i] = p.op("vector", "reduce_sum", st["s1"][:, i:i + 1],
                            x_s[i % 3][:], axis=AX.X, wait=[wl], inc=[(sv, 1)])
            sq_v[i] = p.op("scalar", "activation", out=st["scr"][:],
                           in_=x_s[i % 3][:], func=AF.Square, bias=0.0, scale=1.0,
                           accum_out=st["ssq"][:, i:i + 1],
                           wait=[wl] + ([(ss, sq_v[i - 1])] if i >= 1
                                        else [(sv, v_bias)]),
                           inc=[(ss, 1)])
        p.op("vector", "reduce_sum", st["s1"][0:1, NPT:NPT + 1], xb_s[:], axis=AX.X,
             wait=[w_xb], inc=[(sv, 1)])
        sqb = p.op("scalar", "activation", out=st["scr"][0:1, :], in_=xb_s[:],
                   func=AF.Square, bias=0.0, scale=1.0,
                   accum_out=st["ssq"][0:1, NPT:NPT + 1],
                   wait=[(ss, sq_v[-1])], inc=[(ss, 1)])
        p.barrier("vector", sv)
        v_mean = p.op("vector", "tensor_scalar_mul", st["mean"][:, 0:NPT + 1],
                      st["s1"][:, 0:NPT + 1], 1.0 / D, inc=[(sv, 1)])
        p.wait("vector", sv, v_mean)
        v_ln = emit_ln_stats_finish(p, st, NPT + 1, sv, ss, sq_done=(ss, sqb),
                                    central=False)

        apply_v = [0] * NPT
        evac_v = [0] * NPT
        for i in range(NPT):
            j = NPT - 1 - ((NPT - 1 - i) % 3)
            w = ([(ss, apply_v[i - 3])] if i >= 3
                 else [(ss, sq_v[j]), (sv, red_v[j])])
            wl = p.dma("sync", lx[i % 3],
                       [(x_s[i % 3][:], x_own[1 + 128 * i: 1 + 128 * (i + 1), :])],
                       wait=w)
            w2 = [wl]
            if i == 0:
                w2.append((sv, v_ln))
            if i >= 2:
                w2.append((sv, evac_v[i - 2]))
            apply_v[i] = p.op("scalar", "activation", out=xn_s[i % 2][:], in_=x_s[i % 3][:],
                              func=AF.Identity, bias=st["nmr"][:, i:i + 1],
                              scale=st["rstd"][:, i:i + 1], wait=w2, inc=[(ss, 1)])
            wf = [(ss, apply_v[i]), (sg, g_id)]
            if i >= 1:
                wf.append((sv, evac_v[i - 1]))
            evac_v[i] = emit_transpose8(p, xnt, 1 + 128 * i, xn_s[i % 2],
                                        ident_bf, ps_tr, sm, sv, wait_first=wf)
        s_b = p.op("scalar", "activation", out=xnb_s[:], in_=xb_s[:],
                   func=AF.Identity, bias=st["nmr"][0:1, NPT:NPT + 1],
                   scale=st["rstd"][0:1, NPT:NPT + 1], wait=[(sv, v_ln)], inc=[(ss, 1)])
        for f in range(NF):
            w = [(ss, s_b), (sv, evac_v[NPT - 1])] if f == 0 else []
            tmb = p.op("tensor", "transpose", ps_tr[0][:, 2 * f:2 * f + 1],
                       xnb_s[0:1, 128 * f:128 * (f + 1)], ident_bf[0:1, 0:1],
                       wait=w, inc=([(sm, 1)] if f == NF - 1 else []))
        xnt_done = p.op("vector", "tensor_copy", xnt[:, :, 0:1],
                        ps_tr[0][:, 0:2 * NF].rearrange("p (f q) -> p f q", q=2)[:, :, 0:1],
                        wait=[(sm, tmb)], inc=[(sv, 1)])

        p.wait("gpsimd", sv, xnt_done)
        p.dma("gpsimd", s_fin,
              [(xnt_o[128 * f:128 * (f + 1), :], xnt[:, f, :]) for f in range(NF)])

        ev_grp = []
        rope_use = 0
        vst_v = 0
        kt_st = 0
        pend = None
        tr8_v = [0, 0]
        tr8_last = [0]

        def flush_pend():
            nonlocal pend
            if pend is None:
                return 0
            dstT_, i_, buf_, va_ = pend
            pend = None
            wf = [(sv, va_)]
            if tr8_last[0]:
                wf.append((sv, tr8_last[0]))
            tr8_v[buf_] = emit_transpose8(p, dstT_, 128 * i_, qr[buf_],
                                          ident_bf, ps_tr, sm, sv,
                                          wait_first=wf)
            tr8_last[0] = tr8_v[buf_]
            return tr8_v[buf_]

        for ph in (1, 2, 0):
            wld_v = []
            for f in range(NF):
                w = []
                if f >= 2:
                    w.append((sv, wld_v[f - 2]))
                if f == 0 and ev_grp:
                    w.append((sv, ev_grp[-1]))
                wl = p.dma("sync", lw[f % 2],
                           [(wst[f % 2][:], w_qkv[128 * f:128 * (f + 1),
                                                  D * ph: D * (ph + 1)])], wait=w)
                w2 = [wl]
                if f == 0 and ev_grp:
                    w2.append((sv, ev_grp[-1]))
                cv = p.op("vector", "tensor_copy", wph[:, f, :], wst[f % 2][:],
                          wait=w2, inc=[(sv, 1)])
                wld_v.append(cv)
            w_ready = wld_v[-1]
            for i in range(NPT):
                if ph < 2:
                    w = [(sv, rope_use)] if rope_use else []
                    w_cs = p.dma("sync", lcs,
                                 [(cosb[:], rope_cos[128 * i:128 * (i + 1), :]),
                                  (sinb[:], rope_sin[128 * i:128 * (i + 1), :])],
                                 wait=w)
                elif ph == 2 and i == 0:
                    flush_pend()
                ev_q = [0, 0]
                for ncol in range(2):
                    g = len(ev_grp)
                    pmm = ps_mm[g % 3]
                    for f in range(NF):
                        w = []
                        if f == 0:
                            w = ([(sv, max(w_ready, xnt_done, v_bias))] if g < 3
                                 else [(sv, ev_grp[g - 3]), (sv, w_ready)])
                        p.op("tensor", "matmul", pmm[:],
                             xnt[:, f, 1 + 128 * i: 1 + 128 * (i + 1)],
                             wph[:, f, 512 * ncol: 512 * (ncol + 1)],
                             start=(f == 0), stop=False, wait=w)
                    nc0 = D * ph + 512 * ncol
                    p.op("tensor", "matmul", pmm[:], one_r[0:1, :],
                         bias_h[0:1, nc0:nc0 + 512], start=False, stop=False)
                    tb = p.op("tensor", "matmul", pmm[:], one_r[0:1, :],
                              bias_l[0:1, nc0:nc0 + 512], start=False, stop=True,
                              inc=[(sm, 1)])
                    if ph == 2:
                        w = [(sm, tb)]
                        if i >= 1:
                            w.append((s_v, vst_v))
                        ev = p.op("vector", "tensor_copy",
                                  vt_b[:, 512 * ncol:512 * (ncol + 1)],
                                  pmm[:], wait=w, inc=[(sv, 1)])
                    else:
                        w = [(sm, tb)]
                        if ncol == 0 and rope_use:
                            w.append((sv, rope_use))
                        if not ev_grp:
                            w.append((ss, ss.n))
                        ev = p.op("vector", "tensor_copy",
                                  qf[:, 512 * ncol:512 * (ncol + 1)],
                                  pmm[:], wait=w, inc=[(sv, 1)])
                    ev_grp.append(ev)
                    ev_q[ncol] = ev
                if ph == 2:
                    p.wait("gpsimd", sv, ev_q[1])
                    _, vst_v = p.dma("gpsimd", s_v,
                                     [(v_o[128 * i:128 * (i + 1), :], vt_b[:])])
                    if ph == 2 and i == NPT - 1 and hook is not None:
                        hook(p, [(s_fin, kt_st), (s_v, s_v.n)])
                else:
                    buf = i % 2
                    q3 = qf[:].rearrange("p (h d) -> p h d", h=H)
                    r3 = qr[buf][:].rearrange("p (h d) -> p h d", h=H)
                    c3 = cosb[:].rearrange("p (h j) -> p h j", h=H)
                    s3 = sinb[:].rearrange("p (h j) -> p h j", h=H)
                    t13 = t1m[:].rearrange("p (h j) -> p h j", h=H)
                    t23 = t2m[:].rearrange("p (h j) -> p h j", h=H)
                    t33 = t3m[:].rearrange("p (h j) -> p h j", h=H)
                    t43 = t4m[:].rearrange("p (h j) -> p h j", h=H)
                    p.wait("vector", w_cs[0], w_cs[1])
                    p.wait("vector", sv, ev_q[1])
                    if tr8_v[buf]:
                        p.wait("vector", sv, tr8_v[buf])
                    p.op("vector", "tensor_mul", t13, q3[:, :, 0:32], c3)
                    p.op("vector", "tensor_mul", t23, q3[:, :, 32:64], s3)
                    p.op("vector", "tensor_mul", t33, q3[:, :, 0:32], s3)
                    vm = p.op("vector", "tensor_mul", t43, q3[:, :, 32:64], c3,
                              inc=[(sv, 1)])
                    p.wait("vector", sv, vm)
                    p.op("vector", "tensor_sub", r3[:, :, 0:32], t13, t23)
                    va = p.op("vector", "tensor_add", r3[:, :, 32:64], t33, t43,
                              inc=[(sv, 1)])
                    rope_use = va
                    flush_pend()
                    dstT = qT if ph == 0 else kT
                    pend = (dstT, i, buf, va)
                    if ph == 1 and i == NPT - 1:
                        kt_done = flush_pend()
                        p.wait("gpsimd", sv, kt_done)
                        _, kt_st = p.dma(
                            "gpsimd", s_fin,
                            [(kt_o[128 * f:128 * (f + 1), :], kT[:, f, :])
                             for f in range(NF)])
        flush_pend()
        p.barrier("gpsimd", sv)
        p.dma("gpsimd", s_fin,
              [(qt_o[128 * f:128 * (f + 1), :], qT[:, f, :]) for f in range(NF)])
        p.barrier("gpsimd", s_fin)
        p.barrier("gpsimd", s_v)


def emit_gather_direct(nc, p, pairs, pre=""):
    cc = p.sem(pre + "cc")
    c0 = 0
    for in_ap, out_ap in pairs:
        c0 = p.op("gpsimd", "collective_compute", "AllGather", ALU.bypass,
                  replica_groups=GROUPS, ins=[in_ap.opt()], outs=[out_ap.opt()],
                  inc=[(cc, 1)])
    p.wait("gpsimd", cc, c0)
    return (cc, c0)


def build_gather(tensors):
    nc = bass.Bass("TRN2", target_bir_lowering=False)
    with contextlib.ExitStack() as ctx:
        p = Prog(nc, ctx)
        cc = p.sem("cc")
        dms = p.sems("dm", len(tensors) * 2)
        i = 0
        for name, shape in tensors:
            inp = nc.dram_tensor(name, list(shape), BF16, kind="ExternalInput").ap()
            out = nc.dram_tensor(name + "_g", [4 * shape[0], shape[1]], BF16,
                                 kind="ExternalOutput").ap()
            bi = nc.dram_tensor(name + "_bi", list(shape), BF16, kind="Internal")
            bo = nc.dram_tensor(name + "_bo", [4 * shape[0], shape[1]], BF16,
                                kind="Internal")
            wl = p.dma("gpsimd", dms[2 * i], [(bi.ap(), inp)])
            c0 = p.op("gpsimd", "collective_compute", "AllGather", ALU.bypass,
                      replica_groups=GROUPS, ins=[bi.ap().opt()], outs=[bo.ap().opt()],
                      wait=[wl], inc=[(cc, 1)])
            p.dma("gpsimd", dms[2 * i + 1], [(out, bo.ap())], wait=[(cc, c0)])
            i += 1
        for s in dms:
            p.barrier("gpsimd", s)
        with nc.Block() as block:
            p.build(block)
    return nc


L0ATTN_IO = dict(qt=([D, TOK], BF16), kt_g=([4 * D, TOK], BF16),
                 v_g=([4 * TOK, D], BF16), w_out=([D, D], F32),
                 b_out=([128, NF], F32))
L0ATTN_OUT = dict(st=([D, TOK], F32))


def emit_l0attn(nc, p, ctx, io, pre=""):
    qt_i, ktg, vg = io["qt"], io["kt_g"], io["v_g"]
    w_out, b_out, st_o = io["w_out"], io["b_out"], io["st"]
    if True:
        sb = lambda name, shape, dt=F32: ctx.enter_context(nc.sbuf_tensor(pre + name, shape, dt))
        psa = lambda name, shape, dt=F32: ctx.enter_context(nc.psum_tensor(pre + name, shape, dt))
        qT = sb("qT", [128, NF, TOK], BF16)
        OT = sb("OT", [128, NF, TOK], BF16)
        ktb = [sb(f"ktb{i}", [128, 4, TOK], BF16) for i in range(2)]
        vraw = sb("vraw", [128, KT, 128], BF16)
        vt65 = [sb(f"vt65_{i}", [128, KT, 130], BF16) for i in range(2)]
        pT = [sb(f"pT{i}", [128, 1024], BF16) for i in range(2)]
        wo_bf = sb("wo_bf", [128, NF, D], BF16)
        wst = [sb(f"wst{i}", [128, D]) for i in range(2)]
        bo_s = sb("bo_s", [128, NF])
        ones_bf = sb("ones_bf", [128, 1], BF16)
        ones_r = sb("ones_r", [1, 64])
        dr = sb("dr", [1, 1024])
        bc_sb = sb("bc_sb", [64, 1024])
        ev_sb = [sb(f"ev{i}", [128, 512]) for i in range(2)]
        ps_sc = [psa(f"ps_sc{i}", [128, 1024]) for i in range(2)]
        ps_o1 = psa("ps_o1", [65, 512])
        ps_o2 = psa("ps_o2", [65, 512])
        ps_bc = psa("ps_bc", [64, 512])
        ps_pr = psa("ps_pr", [128, 512])

        sv, ss, sm = (p.sem(pre + n) for n in ("sv", "ss", "sm"))
        l_q = p.sems(pre + "l_q", 2)
        lw = p.sems(pre + "lw", 2)
        l_kt = p.sems(pre + "l_kt", 2)
        l_vt = p.sems(pre + "l_vt", 2)
        s_st = p.sems(pre + "s_st", 2)

        p.op("vector", "memset", ones_bf[:], 1.0)
        v_one = p.op("vector", "memset", ones_r[:], 1.0, inc=[(sv, 1)])

        pairs = [(qT[:, f, :], qt_i[128 * f:128 * (f + 1), :]) for f in range(NF)]
        pairs.append((bo_s[:], b_out))
        w_q = p.dma("sync", l_q[0], pairs)
        wcast_v = []
        for f in range(NF):
            w = [(sv, wcast_v[f - 2])] if f >= 2 else []
            wl = p.dma("sync", lw[f % 2],
                       [(wst[f % 2][:], w_out[128 * f:128 * (f + 1), :])], wait=w)
            cv = p.op("vector", "tensor_copy", wo_bf[:, f, :], wst[f % 2][:],
                      wait=[wl], inc=[(sv, 1)])
            wcast_v.append(cv)

        vraw_free = [0]

        def load_hp(hp, buf, wv):
            p.dma("sync", l_kt[buf],
                  [(ktb[buf][:, rk, :],
                    ktg[D * rk + 128 * hp: D * rk + 128 * hp + 128, :])
                   for rk in range(4)], wait=wv)
            wld = p.dma("gpsimd", l_vt[0],
                        [(vraw[:, 8 * blk:8 * (blk + 1), :],
                          vg[1024 * blk:1024 * (blk + 1), 128 * hp:128 * (hp + 1)]
                          .rearrange("(t p) c -> p t c", p=128))
                         for blk in range(8)],
                        wait=([(sv, vraw_free[0])] if vraw_free[0] else []))
            p.op("vector", "memset", vt65[buf][:, :, 64:65], 1.0, wait=wv)
            p.op("vector", "memset", vt65[buf][:, :, 129:130], 1.0)
            p.op("vector", "tensor_copy", vt65[buf][:, :, 0:64],
                 vraw[:, :, 0:64], wait=[wld])
            cpv = p.op("vector", "tensor_copy", vt65[buf][:, :, 65:129],
                       vraw[:, :, 64:128], inc=[(sv, 1)])
            vraw_free[0] = cpv
            return [(l_kt[buf], l_kt[buf].n), (sv, cpv)]

        hp_ld = [None] * (NF + 1)
        hp_use = [0] * (NF + 1)
        hp_ld[0] = load_hp(0, 0, [])
        norm_v = 0
        for hp in range(NF):
            buf = hp % 2
            if hp + 1 < NF:
                wv = [(sm, hp_use[hp - 1])] if hp >= 1 else []
                hp_ld[hp + 1] = load_hp(hp + 1, 1 - buf, wv)
            for c in range(4):
                exp_v = [0] * KT
                qk_v = [0] * KT
                avdone_v = [0] * KT

                def emit_qk(kt, extra):
                    par = kt % 2
                    ktap = ktb[buf][:, kt // 16,
                                    128 * (kt % 16):128 * (kt % 16) + 128]
                    p.op("tensor", "matmul", ps_sc[par][:, 0:512],
                         ktap[0:64, :], qT[0:64, hp, 512 * c:512 * (c + 1)],
                         start=True, stop=True, tile_position=(0, 0), wait=extra)
                    qk_v[kt] = p.op("tensor", "matmul", ps_sc[par][:, 512:1024],
                                    ktap[64:128, :],
                                    qT[64:128, hp, 512 * c:512 * (c + 1)],
                                    start=True, stop=True, tile_position=(64, 0),
                                    inc=[(sm, 1)])

                w0 = [(sv, v_one)]
                if c == 0:
                    w0 = hp_ld[hp] + [w_q, (sv, v_one)]
                emit_qk(0, w0)
                for kt in range(KT):
                    par = kt % 2
                    we = [(sm, qk_v[kt])]
                    if kt >= 2:
                        we.append((sm, avdone_v[kt - 2]))
                    exp_v[kt] = p.op("scalar", "activation", out=pT[par][:, :],
                                     in_=ps_sc[par][:, :], func=AF.Exp, bias=0.0,
                                     scale=L0_SCALE, wait=we, inc=[(ss, 1)])
                    if kt + 1 < KT:
                        emit_qk(kt + 1,
                                [(ss, exp_v[kt - 1])] if kt >= 1 else [])
                    w = [(ss, exp_v[kt])]
                    if kt == 0:
                        w.append((sv, norm_v))
                    p.op("tensor", "matmul", ps_o1[0:65, :],
                         vt65[buf][:, kt, 0:65], pT[par][:, 0:512],
                         start=(kt == 0), stop=(kt == KT - 1), wait=w)
                    avdone_v[kt] = p.op("tensor", "matmul", ps_o2[0:65, :],
                                        vt65[buf][:, kt, 65:130],
                                        pT[par][:, 512:1024],
                                        start=(kt == 0), stop=(kt == KT - 1),
                                        inc=[(sm, 1)])
                hp_use[hp] = avdone_v[KT - 1]
                v_r = p.op("vector", "reciprocal", dr[0:1, 0:512], ps_o1[64:65, :],
                           wait=[(sm, avdone_v[KT - 1])], inc=[(sv, 1)])
                v_r2 = p.op("vector", "reciprocal", dr[0:1, 512:1024], ps_o2[64:65, :],
                            inc=[(sv, 1)])
                t_b = p.op("tensor", "matmul", ps_bc[:], ones_r[0:1, :],
                           dr[0:1, 0:512], start=True, stop=True,
                           wait=[(sv, v_r2)], inc=[(sm, 1)])
                v_c = p.op("vector", "tensor_copy", bc_sb[:, 0:512], ps_bc[:],
                           wait=[(sm, t_b)], inc=[(sv, 1)])
                t_b2 = p.op("tensor", "matmul", ps_bc[:], ones_r[0:1, :],
                            dr[0:1, 512:1024], start=True, stop=True,
                            wait=[(sv, v_c)], inc=[(sm, 1)])
                v_c2 = p.op("vector", "tensor_copy", bc_sb[:, 512:1024], ps_bc[:],
                            wait=[(sm, t_b2)], inc=[(sv, 1)])
                p.wait("vector", sv, v_c)
                p.op("vector", "tensor_mul", OT[0:64, hp, 512 * c:512 * (c + 1)],
                     ps_o1[0:64, :], bc_sb[:, 0:512])
                p.wait("vector", sv, v_c2)
                norm_v = p.op("vector", "tensor_mul",
                              OT[64:128, hp, 512 * c:512 * (c + 1)],
                              ps_o2[0:64, :], bc_sb[:, 512:1024], inc=[(sv, 1)])
        ev_v = [0, 0]
        stv = [0, 0]
        for m in range(NF):
            for c in range(4):
                j = m * 4 + c
                for f in range(NF):
                    w = []
                    if f == 0:
                        w = [(sv, norm_v)] if j == 0 else [(sv, ev_v[(j - 1) % 2])]
                    tp = p.op("tensor", "matmul", ps_pr[:],
                              wo_bf[:, f, 128 * m:128 * (m + 1)],
                              OT[:, f, 512 * c:512 * (c + 1)],
                              start=(f == 0), stop=(f == NF - 1), wait=w,
                              inc=([(sm, 1)] if f == NF - 1 else []))
                w = [(sm, tp)]
                if j >= 2:
                    w.append((s_st[j % 2], stv[j % 2]))
                ve = p.op("vector", "tensor_scalar_add", ev_sb[j % 2][:], ps_pr[:],
                          bo_s[:, m:m + 1], wait=w, inc=[(sv, 1)])
                ev_v[j % 2] = ve
                _, stv[j % 2] = p.dma(
                    "gpsimd", s_st[j % 2],
                    [(st_o[128 * m:128 * (m + 1), 512 * c:512 * (c + 1)],
                      ev_sb[j % 2][:])], wait=[(sv, ve)])
        p.barrier("gpsimd", s_st[0], s_st[1])


L0RWKV_IO = dict(xnt=([D, TOK + 1], BF16), w_r=([D, D], F32), w_k=([D, D], F32),
                 w_v=([D, D], F32), mu_r=([128, NF], F32), mu_k=([128, NF], F32),
                 st=([D, TOK], F32), x_own=([TOK + 1, D], F32))
L0RWKV_OUT = dict(x1=([TOK, D], F32))


def emit_l0rwkv(nc, p, ctx, io, pre=""):
    xnt_i, w_r, w_k, w_v = io["xnt"], io["w_r"], io["w_k"], io["w_v"]
    mu_r, mu_k, st_i, x_own = io["mu_r"], io["mu_k"], io["st"], io["x_own"]
    x1_o = io["x1"]
    if True:
        sb = lambda name, shape, dt=F32: ctx.enter_context(nc.sbuf_tensor(pre + name, shape, dt))
        psa = lambda name, shape, dt=F32: ctx.enter_context(nc.psum_tensor(pre + name, shape, dt))
        xnt = sb("xnt_s", [128, NF, TOK + 1], BF16)
        wT = {n: sb(f"wT_{n}", [128, NF, D], BF16) for n in ("r", "k", "v")}
        wst = [sb(f"wst{i}", [128, D]) for i in range(2)]
        wbf = sb("wbf", [128, D], BF16)
        mur_s = sb("mur_s", [128, NF])
        muk_s = sb("muk_s", [128, NF])
        tdif = sb("tdif", [128, 512], BF16)
        xr = [sb(f"xr{i}", [128, NF, 512], BF16) for i in range(2)]
        xk = [sb(f"xk{i}", [128, NF, 512], BF16) for i in range(2)]
        rb = sb("rb", [128, NF, 512], BF16)
        kab = sb("kab", [128, NF, 512], BF16)
        gb = sb("gb", [128, 512], BF16)
        stc = sb("stc", [128, NF, 512])
        uT = sb("uT", [128, NF, 512])
        gf = sb("gf", [128, 512])
        x_t = [sb(f"x_t{i}", [128, D]) for i in range(2)]
        x1_t = [sb(f"x1_t{i}", [128, D]) for i in range(2)]
        ident_bf = sb("ident_bf", [128, 128], BF16)
        ident_f = sb("ident_f", [128, 128])
        ps_mm = [psa(f"psmm{i}", [128, 512]) for i in range(3)]
        ps_tr = [psa(f"pstr{i}", [128, 1024], BF16) for i in range(2)]
        ps_tf = [psa(f"pstf{i}", [128, 512]) for i in range(2)]

        sv, ss, sg, sm = (p.sem(pre + n) for n in ("sv", "ss", "sg", "sm"))
        l_xnt = p.sem(pre + "l_xnt")
        lw = p.sems(pre + "lw", 2)
        l_stc = p.sem(pre + "l_stc")
        l_x = p.sems(pre + "l_x", 2)
        s_x1 = p.sems(pre + "s_x1", 2)
        emit_ident(p, ident_bf, sg)
        g_id = emit_ident(p, ident_f, sg)

        pairs = [(xnt[:, f, :], xnt_i[128 * f:128 * (f + 1), :]) for f in range(NF)]
        pairs += [(mur_s[:], mu_r), (muk_s[:], mu_k)]
        w_xnt = p.dma("sync", l_xnt, pairs)

        tm_last = 0
        tr_last = 0
        for (n, wdram) in (("r", w_r), ("k", w_k), ("v", w_v)):
            for m in range(NF):
                w = [(sv, tr_last)] if tr_last else []
                wl = p.dma("sync", lw[m % 2],
                           [(wst[m % 2][:], wdram[128 * m:128 * (m + 1), :])], wait=w)
                cv = p.op("vector", "tensor_copy", wbf[:], wst[m % 2][:],
                          wait=[wl] + ([(sm, tm_last)] if tm_last else []),
                          inc=[(sv, 1)])
                tr_last = emit_transpose8(p, wT[n], 128 * m, wbf, ident_bf,
                                          ps_tr, sm, sv,
                                          wait_first=[(sv, cv), (sg, g_id)] +
                                          ([(sv, tr_last)] if tr_last else []))
                tm_last = sm.n
        w_ready = tr_last

        ev_grp = []
        gf_free = None
        stc_free = 0
        uT_free = 0
        stv = [0, 0]
        mix_done_v = [0] * 4
        rk_done_v = [0] * 4

        def emit_mixes(cch, extra):
            c0 = 512 * cch
            mix_prev = 0
            for f in range(NF):
                w = list(extra) if f == 0 else []
                if mix_prev:
                    w.append((sv, mix_prev))
                vt = p.op("vector", "tensor_sub", tdif[:],
                          xnt[:, f, 1 + c0:1 + c0 + 512], xnt[:, f, c0:c0 + 512],
                          wait=w, inc=[(sv, 1)])
                p.wait("vector", sv, vt)
                p.op("vector", "scalar_tensor_tensor", xr[cch % 2][:, f, :], tdif[:],
                     mur_s[:, f:f + 1], xnt[:, f, c0:c0 + 512],
                     op0=ALU.mult, op1=ALU.add)
                mix_prev = p.op("vector", "scalar_tensor_tensor", xk[cch % 2][:, f, :],
                                tdif[:], muk_s[:, f:f + 1], xnt[:, f, c0:c0 + 512],
                                op0=ALU.mult, op1=ALU.add, inc=[(sv, 1)])
            mix_done_v[cch] = mix_prev

        emit_mixes(0, [w_xnt])
        for cch in range(4):
            c0 = 512 * cch
            w_st = p.dma("sync", l_stc,
                         [(stc[:, f, :], st_i[128 * f:128 * (f + 1), c0:c0 + 512])
                          for f in range(NF)],
                         wait=([(sv, stc_free)] if cch >= 1 else []))
            mix_done = mix_done_v[cch]
            for (src, wn, kind) in ((xr[cch % 2], "r", "sig"),
                                    (xk[cch % 2], "k", "relu2")):
                for m in range(NF):
                    g = len(ev_grp)
                    pmm = ps_mm[g % 3]
                    for f in range(NF):
                        w = []
                        if f == 0:
                            w = ([(sv, max(mix_done, w_ready))] if g < 3
                                 else [(sv, ev_grp[g - 3]), (sv, mix_done)])
                        tb = p.op("tensor", "matmul", pmm[:],
                                  wT[wn][:, f, 128 * m:128 * (m + 1)],
                                  src[:, f, :], start=(f == 0),
                                  stop=(f == NF - 1), wait=w,
                                  inc=([(sm, 1)] if f == NF - 1 else []))
                    if kind == "sig":
                        ev = p.op("scalar", "activation", out=rb[:, m, :], in_=pmm[:],
                                  func=AF.Sigmoid, bias=0.0, scale=1.0,
                                  wait=[(sm, tb)], inc=[(sv, 1)])
                    else:
                        wgf = [(sm, tb)]
                        if gf_free is not None:
                            wgf.append(gf_free)
                        sr = p.op("scalar", "activation", out=gf[:], in_=pmm[:],
                                  func=AF.Relu, bias=0.0, scale=1.0,
                                  wait=wgf, inc=[(ss, 1)])
                        ev = p.op("vector", "tensor_mul", kab[:, m, :], gf[:], gf[:],
                                  wait=[(ss, sr)], inc=[(sv, 1)])
                        gf_free = (sv, ev)
                    ev_grp.append(ev)
            rk_done_v[cch] = sm.n
            if cch + 1 < 4:
                emit_mixes(cch + 1, [(sm, rk_done_v[cch])])
            kact_done = ev_grp[-1]
            u_prev = 0
            for m in range(NF):
                g = len(ev_grp)
                pmm = ps_mm[g % 3]
                for f in range(NF):
                    w = []
                    if f == 0:
                        w = ([(sv, kact_done)] if g < 3
                             else [(sv, ev_grp[g - 3]), (sv, kact_done)])
                    tb = p.op("tensor", "matmul", pmm[:],
                              wT["v"][:, f, 128 * m:128 * (m + 1)],
                              kab[:, f, :], start=(f == 0), stop=(f == NF - 1),
                              wait=w, inc=([(sm, 1)] if f == NF - 1 else []))
                vkv = p.op("vector", "tensor_mul", gf[:], pmm[:], rb[:, m, :],
                           wait=[(sm, tb), gf_free], inc=[(sv, 1)])
                ev_grp.append(vkv)
                sg_ = p.op("scalar", "activation", out=gb[:], in_=gf[:],
                           func=AF.Silu, bias=0.0, scale=1.0,
                           wait=[(sv, vkv)] + ([(sv, u_prev)] if u_prev else []),
                           inc=[(ss, 1)])
                gf_free = (ss, sg_)
                w = [(ss, sg_), w_st]
                if cch > 0 and m == 0:
                    w.append((sm, uT_free))
                u_prev = p.op("vector", "tensor_mul", uT[:, m, :], stc[:, m, :], gb[:],
                              wait=w, inc=[(sv, 1)])
            u_done = u_prev
            stc_free = u_done
            res_v = 0
            tr_done = 0
            for tb4 in range(4):
                ti = 4 * cch + tb4
                wl = p.dma("sync", l_x[tb4 % 2],
                           [(x_t[tb4 % 2][:],
                             x_own[1 + 128 * ti:1 + 128 * (ti + 1), :])],
                           wait=([(sv, res_v)] if ti >= 2 else []))
                for half in range(2):
                    pf = ps_tf[half]
                    for j in range(4):
                        m = 4 * half + j
                        w = []
                        if m == 0 and tb4 == 0:
                            w = [(sv, u_done), (sg, g_id)]
                        if j == 0 and tr_done:
                            w.append((sv, tr_done))
                        p.op("tensor", "transpose",
                             pf[:, 128 * j:128 * (j + 1)],
                             uT[:, m, 128 * tb4:128 * (tb4 + 1)], ident_f[:],
                             wait=w, inc=([(sm, 1)] if j == 3 else []))
                    wv_ = [(sm, sm.n), wl]
                    if half == 0 and ti >= 2:
                        wv_.append((s_x1[tb4 % 2], stv[tb4 % 2]))
                    tr_done = res_v = p.op(
                        "vector", "tensor_add",
                        x1_t[tb4 % 2][:, 512 * half:512 * (half + 1)],
                        x_t[tb4 % 2][:, 512 * half:512 * (half + 1)],
                        pf[:, 0:512], wait=wv_, inc=[(sv, 1)])
                p.wait("gpsimd", sv, res_v)
                _, stv[tb4 % 2] = p.dma(
                    "gpsimd", s_x1[tb4 % 2],
                    [(x1_o[128 * ti:128 * (ti + 1), :], x1_t[tb4 % 2][:])])
            uT_free = sm.n
        p.barrier("gpsimd", s_x1[0], s_x1[1])


L1PRE_IO = dict(x1=([TOK, D], F32), wq1=([D, D], F32), wk1=([D, D], F32),
                wv1=([D, D], F32), wt1=([D, D], F32), alpha1=([1, 1], F32),
                bias1=([128, NF], F32))
L1PRE_OUT = dict(qt2=([D, TOK], BF16), kt2=([D, TOK], BF16),
                 v2=([TOK, D], BF16), g2t=([D, TOK], BF16))


def emit_l1pre(nc, p, ctx, io, pre="", hook=None):
    x1_i, wq, wk, wv, wt = io["x1"], io["wq1"], io["wk1"], io["wv1"], io["wt1"]
    alpha, bias1 = io["alpha1"], io["bias1"]
    qt2_o, kt2_o, v2_o, g2t_o = io["qt2"], io["kt2"], io["v2"], io["g2t"]
    if True:
        sb = lambda name, shape, dt=F32: ctx.enter_context(nc.sbuf_tensor(pre + name, shape, dt))
        psa = lambda name, shape, dt=F32: ctx.enter_context(nc.psum_tensor(pre + name, shape, dt))
        x_s = [sb(f"xs{i}", [128, D]) for i in range(3)]
        xn_s = [sb(f"xn{i}", [128, D], BF16) for i in range(2)]
        xnt = sb("xnt_s", [128, NF, TOK], BF16)
        wq_bf = sb("wq_bf", [128, NF, D], BF16)
        wk_bf = sb("wk_bf", [128, NF, D], BF16)
        wv_bf = sb("wv_bf", [128, NF, D], BF16)
        wt_f = sb("wt_f", [128, NF, D])
        ternT = sb("ternT", [128, NF, D], BF16)
        tern_b = sb("tern_b", [128, D], BF16)
        ta_s = sb("ta_s", [128, D])
        tb_s = sb("tb_s", [128, D])
        wst = [sb(f"wst{i}", [128, D]) for i in range(2)]
        asum = sb("asum", [128, NF])
        atot = sb("atot", [128, 1])
        ones_c = sb("ones_c", [128, 1])
        ones_row = sb("ones_row", [1, 128])
        dcol = sb("dcol", [128, 1])
        ndcol = sb("ndcol", [128, 1])
        acol = sb("acol", [128, 1])
        tot1 = sb("tot1", [1, 1])
        al1 = sb("al1", [1, 1])
        bi_s = sb("bi_s", [128, NF])
        gf = sb("gf", [128, 512])
        evb = [sb(f"evb{i}", [128, 512], BF16) for i in range(2)]
        ident_bf = sb("ident_bf", [128, 128], BF16)
        st = alloc_ln_stats(nc, ctx, NPT, pre=pre)
        ps_mm = [psa(f"psmm{i}", [128, 512]) for i in range(3)]
        ps_tr = [psa(f"pstr{i}", [128, 1024], BF16) for i in range(2)]
        ps_sc1 = psa("ps_sc1", [1, 512])
        ps_bc1 = psa("ps_bc1", [128, 512])

        sv, ss, sg, sm = (p.sem(pre + n) for n in ("sv", "ss", "sg", "sm"))
        lx = p.sems(pre + "lx", 3)
        lmisc = p.sem(pre + "lmisc")
        lw = p.sems(pre + "lw", 2)
        s_ev = p.sems(pre + "s_ev", 2)
        g_id = emit_ident(p, ident_bf, sg)

        p.op("vector", "memset", st["s1"][:], 0.0)
        p.op("vector", "memset", st["ssq"][:], 0.0)
        p.op("vector", "memset", ones_c[:], 1.0)
        p.op("vector", "memset", ones_row[:], 1.0)
        v0 = p.op("vector", "memset", st["eps"][:], 1e-5, inc=[(sv, 1)])
        p.wait("vector", sv, v0)
        w_misc = p.dma("sync", lmisc, [(bi_s[:], bias1), (al1[:], alpha)])

        wc_v = []
        for (dst, src) in ((wq_bf, wq), (wk_bf, wk), (wv_bf, wv), (wt_f, wt)):
            for f in range(NF):
                w = [(sv, wc_v[-2])] if len(wc_v) >= 2 else []
                wl = p.dma("sync", lw[f % 2],
                           [(wst[f % 2][:], src[128 * f:128 * (f + 1), :])], wait=w)
                cv = p.op("vector", "tensor_copy", dst[:, f, :], wst[f % 2][:],
                          wait=[wl], inc=[(sv, 1)])
                wc_v.append(cv)
        w_ready = wc_v[-1]

        p.wait("vector", sv, w_ready)
        for f in range(NF):
            p.op("vector", "tensor_reduce", asum[:, f:f + 1], wt_f[:, f, :],
                 axis=AX.X, op=ALU.add, apply_absolute_value=True)
        va = p.op("vector", "memset", atot[:], 0.0, inc=[(sv, 1)])
        p.wait("vector", sv, va)
        vaa = p.op("vector", "tensor_reduce", atot[:, 0:1], asum[:, 0:NF],
                   axis=AX.X, op=ALU.add, inc=[(sv, 1)])
        tsum = p.op("tensor", "matmul", ps_sc1[:, 0:1], atot[:, 0:1], ones_c[:, 0:1],
                    start=True, stop=True, wait=[(sv, vaa)], inc=[(sm, 1)])
        vts = p.op("vector", "tensor_copy", tot1[:], ps_sc1[:, 0:1], wait=[(sm, tsum)],
                   inc=[(sv, 1)])
        p.wait("vector", sv, vts)
        vsc = p.op("vector", "tensor_scalar_mul", tot1[:], tot1[:], 0.7 / (D * D),
                   inc=[(sv, 1)])
        tb1 = p.op("tensor", "matmul", ps_bc1[:, 0:1], ones_row[0:1, :], tot1[0:1, 0:1],
                   start=True, stop=True, wait=[(sv, vsc)], inc=[(sm, 1)])
        vd = p.op("vector", "tensor_copy", dcol[:], ps_bc1[:, 0:1], wait=[(sm, tb1)],
                  inc=[(sv, 1)])
        p.wait("vector", sv, vd)
        vnd = p.op("vector", "tensor_scalar_mul", ndcol[:], dcol[:], -1.0, inc=[(sv, 1)])
        tb2 = p.op("tensor", "matmul", ps_bc1[:, 0:1], ones_row[0:1, :], al1[0:1, 0:1],
                   start=True, stop=True, wait=[(sv, vd), w_misc], inc=[(sm, 1)])
        vac = p.op("vector", "tensor_copy", acol[:], ps_bc1[:, 0:1], wait=[(sm, tb2)],
                   inc=[(sv, 1)])

        tr_last = 0
        for f in range(NF):
            w = [(sv, vnd)] if f == 0 else [(sv, tr_last)]
            p.op("vector", "tensor_single_scalar", ta_s[:], wt_f[:, f, :],
                 dcol[:, 0:1], ALU.is_gt, wait=w)
            vg_ = p.op("vector", "tensor_single_scalar", tb_s[:], wt_f[:, f, :],
                       ndcol[:, 0:1], ALU.is_lt, inc=[(sv, 1)])
            p.wait("vector", sv, vg_)
            vt_ = p.op("vector", "tensor_sub", tern_b[:], ta_s[:], tb_s[:], inc=[(sv, 1)])
            tr_last = emit_transpose8(p, ternT, 128 * f, tern_b, ident_bf,
                                      ps_tr, sm, sv,
                                      wait_first=[(sv, vt_), (sg, g_id)] +
                                      ([(sv, tr_last)] if tr_last else []))
        tern_ready = tr_last

        red_v = [0] * NPT
        sq_v = [0] * NPT
        for i in range(NPT):
            w = [(sv, red_v[i - 3]), (ss, sq_v[i - 3])] if i >= 3 else []
            wl = p.dma("sync", lx[i % 3],
                       [(x_s[i % 3][:], x1_i[128 * i:128 * (i + 1), :])], wait=w)
            red_v[i] = p.op("vector", "reduce_sum", st["s1"][:, i:i + 1],
                            x_s[i % 3][:], axis=AX.X, wait=[wl], inc=[(sv, 1)])
            sq_v[i] = p.op("scalar", "activation", out=st["scr"][:],
                           in_=x_s[i % 3][:], func=AF.Square, bias=0.0, scale=1.0,
                           accum_out=st["ssq"][:, i:i + 1],
                           wait=[wl] + ([(ss, sq_v[i - 1])] if i >= 1 else []),
                           inc=[(ss, 1)])
        p.barrier("vector", sv)
        v_mean = p.op("vector", "tensor_scalar_mul", st["mean"][:, 0:NPT],
                      st["s1"][:, 0:NPT], 1.0 / D, inc=[(sv, 1)])
        p.wait("vector", sv, v_mean)
        v_ln = emit_ln_stats_finish(p, st, NPT, sv, ss, sq_done=(ss, sq_v[-1]),
                                    central=False)
        apply_v = [0] * NPT
        evac_v = [0] * NPT
        for i in range(NPT):
            j = NPT - 1 - ((NPT - 1 - i) % 3)
            w = ([(ss, apply_v[i - 3])] if i >= 3
                 else [(ss, sq_v[j]), (sv, red_v[j])])
            wl = p.dma("sync", lx[i % 3],
                       [(x_s[i % 3][:], x1_i[128 * i:128 * (i + 1), :])], wait=w)
            w2 = [wl]
            if i == 0:
                w2.append((sv, v_ln))
            if i >= 2:
                w2.append((sv, evac_v[i - 2]))
            apply_v[i] = p.op("scalar", "activation", out=xn_s[i % 2][:],
                              in_=x_s[i % 3][:],
                              func=AF.Identity, bias=st["nmr"][:, i:i + 1],
                              scale=st["rstd"][:, i:i + 1], wait=w2, inc=[(ss, 1)])
            wf = [(ss, apply_v[i]), (sg, g_id)]
            if i >= 1:
                wf.append((sv, evac_v[i - 1]))
            evac_v[i] = emit_transpose8(p, xnt, 128 * i, xn_s[i % 2],
                                        ident_bf, ps_tr, sm, sv, wait_first=wf)
        xnt_done = evac_v[-1]


        ev_grp = []
        stv = [0, 0]
        cnt = [0]

        def proj_fm(lhs3, dma_out, ready, epilogue):
            for m in range(NF):
                for cch in range(4):
                    g = len(ev_grp)
                    pmm = ps_mm[g % 3]
                    for f in range(NF):
                        w = []
                        if f == 0:
                            w = ([(sv, ready)] if g < 3
                                 else [(sv, ev_grp[g - 3]), (sv, ready)])
                        tb = p.op("tensor", "matmul", pmm[:],
                                  lhs3[:, f, 128 * m:128 * (m + 1)],
                                  xnt[:, f, 512 * cch:512 * (cch + 1)],
                                  start=(f == 0), stop=(f == NF - 1), wait=w,
                                  inc=([(sm, 1)] if f == NF - 1 else []))
                    j = cnt[0]
                    wv_ = [(sm, tb)]
                    if j >= 2:
                        wv_.append((s_ev[j % 2], stv[j % 2]))
                    esem, ev = epilogue(pmm, m, j % 2, wv_)
                    ev_grp.append(ev if esem is sv else sv.n)
                    _, stv[j % 2] = p.dma(
                        "gpsimd", s_ev[j % 2],
                        [(dma_out[128 * m:128 * (m + 1), 512 * cch:512 * (cch + 1)],
                          evb[j % 2][:])], wait=[(esem, ev)])
                    cnt[0] += 1

        def copy_ep(pmm, m, par, wv_):
            return sv, p.op("vector", "tensor_copy", evb[par][:], pmm[:], wait=wv_,
                            inc=[(sv, 1)])

        gf_free = [0]

        def g2_ep(pmm, m, par, wv_):
            if gf_free[0]:
                wv_ = wv_ + [(ss, gf_free[0])]
            vv = p.op("vector", "tensor_scalar", gf[:], pmm[:], acol[:, 0:1],
                      bi_s[:, m:m + 1], op0=ALU.mult, op1=ALU.add,
                      wait=wv_ + [(sv, vac)], inc=[(sv, 1)])
            r = p.op("scalar", "activation", out=evb[par][:], in_=gf[:],
                     func=AF.Silu, bias=0.0, scale=1.0, wait=[(sv, vv)],
                     inc=[(ss, 1)])
            gf_free[0] = r
            return ss, r

        proj_fm(wk_bf, kt2_o, max(xnt_done, w_ready), copy_ep)
        for ti in range(NPT):
            for dv in range(2):
                g = len(ev_grp)
                pmm = ps_mm[g % 3]
                for f in range(NF):
                    w = []
                    if f == 0:
                        w = [(sv, ev_grp[g - 3])] if g >= 3 else []
                    tb = p.op("tensor", "matmul", pmm[:],
                              xnt[:, f, 128 * ti:128 * (ti + 1)],
                              wv_bf[:, f, 512 * dv:512 * (dv + 1)],
                              start=(f == 0), stop=(f == NF - 1), wait=w,
                              inc=([(sm, 1)] if f == NF - 1 else []))
                j = cnt[0]
                wv_ = [(sm, tb)]
                if j >= 2:
                    wv_.append((s_ev[j % 2], stv[j % 2]))
                ev = p.op("vector", "tensor_copy", evb[j % 2][:], pmm[:], wait=wv_,
                          inc=[(sv, 1)])
                ev_grp.append(ev)
                _, stv[j % 2] = p.dma(
                    "gpsimd", s_ev[j % 2],
                    [(v2_o[128 * ti:128 * (ti + 1), 512 * dv:512 * (dv + 1)],
                      evb[j % 2][:])], wait=[(sv, ev)])
                cnt[0] += 1
        if hook is not None:
            hook(p, [(s_ev[0], s_ev[0].n), (s_ev[1], s_ev[1].n)])
        proj_fm(wq_bf, qt2_o, max(xnt_done, w_ready), copy_ep)
        proj_fm(ternT, g2t_o, tern_ready, g2_ep)

        p.barrier("gpsimd", s_ev[0], s_ev[1])


def _ktg_slice(ktg, f, ktblk):
    rk = ktblk // 2
    return ktg[D * rk + 128 * f: D * rk + 128 * (f + 1),
               1024 * (ktblk % 2):1024 * (ktblk % 2) + 1024]


L1ATTN_IO = dict(qt2=([D, TOK], BF16), kt2_g=([4 * D, TOK], BF16),
                 v2_g=([4 * TOK, D], BF16), wo1=([D, D], F32),
                 g2t=([D, TOK], BF16), x1=([TOK, D], F32),
                 mask=([16, 8, 128, 8, 128], BF16))
L1ATTN_OUT = dict(out=([TOK, D], F32))


def emit_l1attn(nc, p, ctx, io, pre=""):
    qt2, ktg, vg, wo = io["qt2"], io["kt2_g"], io["v2_g"], io["wo1"]
    g2t, x1_i, mask, out_o = io["g2t"], io["x1"], io["mask"], io["out"]
    if True:
        sb = lambda name, shape, dt=F32: ctx.enter_context(nc.sbuf_tensor(pre + name, shape, dt))
        psa = lambda name, shape, dt=F32: ctx.enter_context(nc.psum_tensor(pre + name, shape, dt))
        qiq = [sb(f"qiq{i}", [128, NF, 512], BF16) for i in range(2)]
        ktb = [sb(f"ktb{i}", [128, NF, 1024], BF16) for i in range(2)]
        vtb = [sb(f"vtb{i}", [128, 8, 512], BF16) for i in range(2)]
        mkb = sb("mkb", [128, 4, 8, 128], BF16)
        PT = sb("PT", [128, KT, 512], BF16)
        pq = sb("pq", [128, 512], BF16)
        wo_bf = sb("wo_bf", [128, NF, D], BF16)
        wst = [sb("wst0", [128, D])]
        ones_bf = sb("ones_bf", [128, 1], BF16)
        dsb = sb("dsb", [128, 4])
        drc = sb("drc", [128, 4])
        o_bf3 = sb("o_bf3", [128, 4, D], BF16)
        oT = sb("oT", [128, NF, 512], BF16)
        g2c = sb("g2c", [128, NF, 512], BF16)
        s2c = sb("s2c", [128, 512])
        uT = sb("uT", [128, NF, 512])
        x_t = [sb("x_t0", [128, D])]
        xo_t = [sb("xo_t0", [128, D])]
        ident_bf = sb("ident_bf", [128, 128], BF16)
        ident_f = sb("ident_f", [128, 128])
        ps_sc = [psa(f"ps_sc{i}", [128, 512]) for i in range(2)]
        ps_o = [psa(f"ps_o{i}", [128, 512]) for i in range(4)]
        ps_tb = [psa(f"ps_tb{i}", [128, 1024], BF16) for i in range(2)]

        sv, ss, sg, sm = (p.sem(pre + n) for n in ("sv", "ss", "sg", "sm"))
        l_w = p.sems(pre + "l_w", 2)
        l_q = p.sems(pre + "l_q", 2)
        l_kt = p.sems(pre + "l_kt", 2)
        l_mk = p.sem(pre + "l_mk")
        l_vt = p.sems(pre + "l_vt", 2)
        l_g2 = p.sem(pre + "l_g2")
        l_x = p.sems(pre + "l_x", 2)
        s_o = p.sems(pre + "s_o", 1)
        emit_ident(p, ident_bf, sg)
        g_id = emit_ident(p, ident_f, sg)
        v_one = p.op("vector", "memset", ones_bf[:], 1.0, inc=[(sv, 1)])

        wc_v = []
        for f in range(NF):
            w = [(sv, wc_v[-1])] if wc_v else []
            wl = p.dma("sync", l_w[0],
                       [(wst[0][:], wo[128 * f:128 * (f + 1), :])], wait=w)
            cv = p.op("vector", "tensor_copy", wo_bf[:, f, :], wst[0][:],
                      wait=[wl], inc=[(sv, 1)])
            wc_v.append(cv)
        wo_ready = wc_v[-1]

        stv = [0, 0]
        q_ld = [None] * 5
        qk_fin = [0] * 4

        def load_q(j, wq_):
            return p.dma("sync", l_q[j % 2],
                         [(qiq[j % 2][:, f, :],
                           qt2[128 * f:128 * (f + 1), 512 * j:512 * (j + 1)])
                          for f in range(NF)], wait=wq_)

        q_ld[0] = load_q(0, [])
        for iq in range(4):
            w_q = q_ld[iq]
            exp_v = [0] * KT
            msk_v = [0] * KT
            qk_v = [0] * KT
            blk_rd = [0] * 9
            blk_ld = [None] * 9
            den_last = 0

            def load_blk(b, wk_):
                return p.dma("sync", l_kt[b % 2],
                             [(ktb[b % 2][:, f, :], _ktg_slice(ktg, f, b))
                              for f in range(NF)], wait=wk_)

            blk_ld[0] = load_blk(0, [(sm, sm.n)] if iq > 0 else [])

            def emit_qk1(kt, extra):
                par = kt % 2
                ktl = kt % 8
                for f in range(NF):
                    w = list(extra) if f == 0 else []
                    qk_v[kt] = p.op("tensor", "matmul", ps_sc[par][:],
                                    ktb[(kt // 8) % 2][:, f,
                                                       128 * ktl:128 * (ktl + 1)],
                                    qiq[iq % 2][:, f, :], start=(f == 0),
                                    stop=(f == NF - 1), wait=w,
                                    inc=([(sm, 1)] if f == NF - 1 else []))

            w0 = [w_q, blk_ld[0]]
            if iq > 0:
                w0.append((sv, sv.n))
            emit_qk1(0, w0)
            for ktblk in range(8):
                if ktblk + 1 < 8:
                    blk_ld[ktblk + 1] = load_blk(
                        ktblk + 1,
                        [(sm, blk_rd[ktblk - 1])] if ktblk >= 1 else [])
                wm_ = []
                if ktblk > 0:
                    wm_ = [(sv, msk_v[8 * ktblk - 1])]
                elif iq > 0:
                    wm_ = [(sv, sv.n)]
                w_mk = p.dma("gpsimd", l_mk,
                             [(mkb[:, ii, :, :].rearrange("p t q -> p (t q)"),
                               mask[4 * iq + ii, ktblk].rearrange("p t q -> p (t q)"))
                              for ii in range(4)], wait=wm_)
                for ktl in range(8):
                    kt = 8 * ktblk + ktl
                    par = kt % 2
                    we = [(sm, qk_v[kt])]
                    if kt >= 1:
                        we.append((sv, msk_v[kt - 1]))
                    exp_v[kt] = p.op("scalar", "activation", out=pq[:],
                                     in_=ps_sc[par][:],
                                     func=AF.Exp, bias=0.0, scale=L1_SCALE,
                                     wait=we, inc=[(ss, 1)])
                    if kt + 1 < KT:
                        wq2 = [(ss, exp_v[kt - 1])] if kt >= 1 else []
                        if (kt + 1) % 8 == 0:
                            wq2.append(blk_ld[(kt + 1) // 8])
                        emit_qk1(kt + 1, wq2)
                        if (kt + 1) % 8 == 7:
                            blk_rd[(kt + 1) // 8] = qk_v[kt + 1]
                    msk_v[kt] = p.op(
                        "vector", "tensor_mul",
                        PT[:, kt, :].rearrange("p (i q) -> p i q", i=4),
                        pq[:].rearrange("p (i q) -> p i q", i=4),
                        mkb[:, :, ktl, :],
                        wait=[(ss, exp_v[kt]), w_mk], inc=[(sv, 1)])
                    for ii in range(4):
                        w = [(sv, msk_v[kt]), (sv, v_one)] if ii == 0 else []
                        den_last = p.op("tensor", "matmul", ps_o[0][:, ii:ii + 1],
                                        PT[:, kt, 128 * ii:128 * (ii + 1)],
                                        ones_bf[:, 0:1],
                                        start=(kt == 0 and ii == 0),
                                        stop=(kt == KT - 1 and ii == 3),
                                        wait=w,
                                        inc=([(sm, 1)] if ii == 3 else []))
            qk_fin[iq] = qk_v[KT - 1]
            if iq + 1 < 4:
                q_ld[iq + 1] = load_q(iq + 1,
                                      [(sm, qk_fin[iq - 1])] if iq >= 1 else [])
            vds = p.op("vector", "tensor_copy", dsb[:], ps_o[0][:, 0:4],
                       wait=[(sm, den_last)], inc=[(sv, 1)])
            vdr = p.op("vector", "reciprocal", drc[:], dsb[:], wait=[(sv, vds)],
                       inc=[(sv, 1)])
            norm_last = 0
            blkav = [0] * 16
            vld = [None] * 16

            def load_vt(j):
                dvh_, kb_ = divmod(j, 8)
                w = [(sm, blkav[j - 2])] if j >= 2 else []
                return p.dma("gpsimd", l_vt[j % 2],
                             [(vtb[j % 2][:],
                               vg[1024 * kb_:1024 * (kb_ + 1),
                                  512 * dvh_:512 * (dvh_ + 1)]
                               .rearrange("(t p) c -> p t c", p=128))], wait=w)

            vld[0] = load_vt(0)
            vld[1] = load_vt(1)
            for dvh in range(2):
                for ktblk in range(8):
                    j = 8 * dvh + ktblk
                    bb = j % 2
                    av_last = 0
                    for ktl in range(8):
                        kt = 8 * ktblk + ktl
                        for ii in range(4):
                            w = []
                            if ktl == 0 and ii == 0:
                                w = [vld[j]]
                                if ktblk == 0:
                                    w.append((sv, norm_last))
                                    w.append((sv, vds))
                                    if dvh == 0:
                                        w.append((sv, msk_v[KT - 1]))
                            av_last = p.op(
                                "tensor", "matmul", ps_o[ii][:],
                                PT[:, kt, 128 * ii:128 * (ii + 1)],
                                vtb[bb][:, ktl, :],
                                start=(kt == 0), stop=(kt == KT - 1), wait=w,
                                inc=([(sm, 1)] if (ktl == 7 and ii == 3) else []))
                    blkav[j] = av_last
                    if j + 2 < 16:
                        vld[j + 2] = load_vt(j + 2)
                for ii in range(4):
                    w = [(sm, blkav[8 * dvh + 7]), (sv, vdr)] if ii == 0 else []
                    norm_last = p.op("vector", "tensor_scalar_mul",
                                     o_bf3[:, ii, 512 * dvh:512 * (dvh + 1)],
                                     ps_o[ii][:], drc[:, ii:ii + 1], wait=w,
                                     inc=[(sv, 1)])
            o_done = norm_last
            oT_done = 0
            for ii in range(4):
                wf = [(sg, g_id), (sv, o_done)]
                if oT_done:
                    wf.append((sv, oT_done))
                oT_done = emit_transpose8(p, oT, 128 * ii, o_bf3[:, ii, :],
                                          ident_bf, ps_tb, sm, sv, wait_first=wf)
            w_g2 = p.dma("sync", l_g2,
                         [(g2c[:, f, :],
                           g2t[128 * f:128 * (f + 1), 512 * iq:512 * (iq + 1)])
                          for f in range(NF)],
                         wait=([(sv, sv.n)] if iq > 0 else []))
            u_done = 0
            pr_prev = [0, 0]
            for m in range(NF):
                for f in range(NF):
                    w = []
                    if f == 0:
                        w = [(sv, max(oT_done, wo_ready))]
                        if pr_prev[m % 2]:
                            w.append((sv, pr_prev[m % 2]))
                    tp = p.op("tensor", "matmul", ps_sc[m % 2][:],
                              wo_bf[:, f, 128 * m:128 * (m + 1)], oT[:, f, :],
                              start=(f == 0), stop=(f == NF - 1), wait=w,
                              inc=([(sm, 1)] if f == NF - 1 else []))
                vs2 = p.op("vector", "tensor_copy", s2c[:], ps_sc[m % 2][:],
                           wait=[(sm, tp)] + ([(sv, u_done)] if u_done else []),
                           inc=[(sv, 1)])
                u_done = p.op("vector", "tensor_mul", uT[:, m, :], s2c[:],
                              g2c[:, m, :],
                              wait=[(sv, vs2), w_g2] +
                                   ([(sm, sm.n)] if (iq > 0 and m == 0) else []),
                              inc=[(sv, 1)])
                pr_prev[m % 2] = vs2
            res_v = 0
            tr_done = 0
            for tb4 in range(4):
                ti = 4 * iq + tb4
                wl = p.dma("sync", l_x[0],
                           [(x_t[0][:], x1_i[128 * ti:128 * (ti + 1), :])],
                           wait=([(sv, res_v)] if ti >= 1 else []))
                for half in range(2):
                    pf = ps_o[half]
                    for j in range(4):
                        m = 4 * half + j
                        w = []
                        if m == 0 and tb4 == 0:
                            w = [(sv, u_done), (sg, g_id)]
                        if j == 0 and tr_done:
                            w.append((sv, tr_done))
                        p.op("tensor", "transpose",
                             pf[:, 128 * j:128 * (j + 1)],
                             uT[:, m, 128 * tb4:128 * (tb4 + 1)], ident_f[:],
                             wait=w, inc=([(sm, 1)] if j == 3 else []))
                    wv_ = [(sm, sm.n), wl]
                    if half == 0 and ti >= 1:
                        wv_.append((s_o[0], stv[0]))
                    tr_done = res_v = p.op(
                        "vector", "tensor_add",
                        xo_t[0][:, 512 * half:512 * (half + 1)],
                        x_t[0][:, 512 * half:512 * (half + 1)],
                        pf[:, 0:512], wait=wv_, inc=[(sv, 1)])
                p.wait("gpsimd", sv, res_v)
                _, stv[0] = p.dma(
                    "gpsimd", s_o[0],
                    [(out_o[128 * ti:128 * (ti + 1), :], xo_t[0][:])])
        p.barrier("gpsimd", s_o[0])


def _mk_io(nc, specs, kind):
    return {name: nc.dram_tensor(name, list(shape), dt, kind=kind).ap()
            for name, (shape, dt) in specs.items()}


def _standalone(emit, ins, outs):
    nc = bass.Bass("TRN2", target_bir_lowering=False)
    io = _mk_io(nc, ins, "ExternalInput")
    io.update(_mk_io(nc, outs, "ExternalOutput"))
    with contextlib.ExitStack() as ctx:
        p = Prog(nc, ctx)
        emit(nc, p, ctx, io)
        with nc.Block() as block:
            p.build(block)
    return nc


def build_l0pre():
    return _standalone(emit_l0pre, L0PRE_IO, L0PRE_OUT)


def build_l0attn():
    return _standalone(emit_l0attn, L0ATTN_IO, L0ATTN_OUT)


def build_l0rwkv():
    return _standalone(emit_l0rwkv, L0RWKV_IO, L0RWKV_OUT)


def build_l1pre():
    return _standalone(emit_l1pre, L1PRE_IO, L1PRE_OUT)


def build_l1attn():
    return _standalone(emit_l1attn, L1ATTN_IO, L1ATTN_OUT)


def build_mega():
    nc = bass.Bass("TRN2", target_bir_lowering=False)
    ext = {}
    for specs in (L0PRE_IO, L0RWKV_IO, L1PRE_IO, L1ATTN_IO, L0ATTN_IO):
        for name, (shape, dt) in specs.items():
            if name in ext or name in ("xnt", "st", "x1", "qt", "kt_g", "v_g",
                                       "qt2", "kt2_g", "v2_g", "g2t"):
                continue
            ext[name] = nc.dram_tensor(name, list(shape), dt,
                                       kind="ExternalInput").ap()
    itn = {}
    for name, (shape, dt) in dict(
            xnt=([D, TOK + 1], BF16), qt=([D, TOK], BF16), kt=([D, TOK], BF16),
            v=([TOK, D], BF16), kt_g=([4 * D, TOK], BF16),
            v_g=([4 * TOK, D], BF16), st=([D, TOK], F32), x1=([TOK, D], F32),
            qt2=([D, TOK], BF16), kt2=([D, TOK], BF16), v2=([TOK, D], BF16),
            g2t=([D, TOK], BF16), kt2_g=([4 * D, TOK], BF16),
            v2_g=([4 * TOK, D], BF16)).items():
        itn[name] = nc.dram_tensor(name, list(shape), dt, kind="Internal").ap()
    out = nc.dram_tensor("out", [TOK, D], F32, kind="ExternalOutput").ap()
    io = dict(ext)
    io.update(itn)
    io["out"] = out
    with contextlib.ExitStack() as octx:
        p = Prog(nc, octx)

        def hook0(pp, waits):
            cc = pp.sem("b_cc")
            for in_ap, out_ap in ((itn["kt"], itn["kt_g"]), (itn["v"], itn["v_g"])):
                pp.op("gpsimd", "collective_compute", "AllGather", ALU.bypass,
                      replica_groups=GROUPS, ins=[in_ap.opt()],
                      outs=[out_ap.opt()], wait=waits, inc=[(cc, 1)])
                waits = []

        def hook1(pp, waits):
            cc = pp.sem("f_cc")
            for in_ap, out_ap in ((itn["kt2"], itn["kt2_g"]),
                                  (itn["v2"], itn["v2_g"])):
                pp.op("gpsimd", "collective_compute", "AllGather", ALU.bypass,
                      replica_groups=GROUPS, ins=[in_ap.opt()],
                      outs=[out_ap.opt()], wait=waits, inc=[(cc, 1)])
                waits = []

        with contextlib.ExitStack() as c1:
            emit_l0pre(nc, p, c1, io, pre="a_", hook=hook0)
            p.phase_barrier()
        with contextlib.ExitStack() as c3:
            emit_l0attn(nc, p, c3, io, pre="c_")
            p.phase_barrier()
        with contextlib.ExitStack() as c4:
            emit_l0rwkv(nc, p, c4, io, pre="d_")
            p.phase_barrier()
        with contextlib.ExitStack() as c5:
            emit_l1pre(nc, p, c5, io, pre="e_", hook=hook1)
            p.phase_barrier()
        with contextlib.ExitStack() as c7:
            emit_l1attn(nc, p, c7, io, pre="g_")
        with nc.Block() as block:
            p.build(block)
    return nc


_BUILDERS = {}


def _get(name, fn):
    if name not in _BUILDERS:
        _BUILDERS[name] = fn()
    return _BUILDERS[name]


def np_split_inputs(inputs):
    x = np.ascontiguousarray(inputs["L0_attn_x"])
    half = 32
    freqs = (1.0 / 500.0 ** (np.arange(half, dtype=np.float64) / half))
    maps = []
    for c in range(8):
        b, r = divmod(c, 4)
        t0 = r * TOK
        xw = np.zeros((TOK + 1, D), np.float32)
        xw[1:] = x[b, t0:t0 + TOK]
        if r > 0:
            xw[0] = x[b, t0 - 1]
        pos = np.arange(t0, t0 + TOK, dtype=np.float64)
        ang = pos[:, None] * freqs[None, :]
        cos_t = np.ascontiguousarray(np.tile(np.cos(ang), (1, H)).astype(np.float32))
        sin_t = np.ascontiguousarray(np.tile(np.sin(ang), (1, H)).astype(np.float32))
        qg = (t0 + np.arange(TOK))
        kg = np.arange(L)
        mq = (kg[None, :] <= qg[:, None])
        mkk = mq.T.astype(NPBF)
        mkk = mkk.reshape(64, 128, 16, 128).transpose(2, 0, 1, 3)
        mkk = mkk.reshape(16, 8, 8, 128, 128).transpose(0, 1, 3, 2, 4)
        maps.append(dict(
            x_own=xw, rope_cos=cos_t, rope_sin=sin_t,
            mask=np.ascontiguousarray(mkk),
            w_qkv=inputs["L0_attn_w_qkv"],
            b_qkv=np.ascontiguousarray(inputs["L0_attn_b_qkv"].reshape(1, 3 * D)),
            w_out=inputs["L0_attn_w_out"],
            b_out=np.ascontiguousarray(inputs["L0_attn_b_out"].reshape(NF, 128).T),
            w_r=inputs["L0_ffn_W_r"], w_k=inputs["L0_ffn_W_k"],
            w_v=inputs["L0_ffn_W_v"],
            mu_r=np.ascontiguousarray(inputs["L0_ffn_mu_r"].reshape(NF, 128).T),
            mu_k=np.ascontiguousarray(inputs["L0_ffn_mu_k"].reshape(NF, 128).T),
            wq1=inputs["L1_attn_Wq"], wk1=inputs["L1_attn_Wk"],
            wv1=inputs["L1_attn_Wv"], wo1=inputs["L1_attn_Wo"],
            wt1=inputs["L1_ffn_weight"],
            alpha1=np.ascontiguousarray(inputs["L1_ffn_alpha"].reshape(1, 1)),
            bias1=np.ascontiguousarray(inputs["L1_ffn_bias"].reshape(NF, 128).T),
        ))
    return maps


def kernel(**inputs):
    maps = np_split_inputs(inputs)
    core_ids = list(range(8))
    nc = _get("mega", build_mega)
    in_maps = [dict(x_own=m["x_own"], w_qkv=m["w_qkv"], b_qkv=m["b_qkv"],
                    rope_cos=m["rope_cos"], rope_sin=m["rope_sin"],
                    w_out=m["w_out"], b_out=m["b_out"], w_r=m["w_r"],
                    w_k=m["w_k"], w_v=m["w_v"], mu_r=m["mu_r"], mu_k=m["mu_k"],
                    wq1=m["wq1"], wk1=m["wk1"], wv1=m["wv1"], wt1=m["wt1"],
                    alpha1=m["alpha1"], bias1=m["bias1"], wo1=m["wo1"],
                    mask=m["mask"]) for m in maps]
    res = run_bass_kernel_spmd(nc, in_maps, core_ids=core_ids)
    out = np.empty((B, L, D), np.float32)
    for c in core_ids:
        b, r = divmod(c, 4)
        out[b, r * TOK:(r + 1) * TOK, :] = res.results[c]["out"]
    return out


# version 12 (final)
# speedup vs baseline: 1.2553x; 1.2553x over previous
import contextlib
import numpy as np
import ml_dtypes
import concourse.bass as bass
import concourse.mybir as mybir
from concourse.bass_utils import run_bass_kernel_spmd

F32 = mybir.dt.float32
BF16 = mybir.dt.bfloat16
AF = mybir.ActivationFunctionType
ALU = mybir.AluOpType
AX = mybir.AxisListType
NPBF = ml_dtypes.bfloat16

B, L, D = 2, 8192, 1024
TOK = 2048
NPT = TOK // 128
NF = D // 128
H, DH = 16, 64
KT = L // 128
GROUPS = [[0, 1, 2, 3], [4, 5, 6, 7]]
L0_SCALE = 1.0 / 8.0
L1_SCALE = 1.0 / 32.0


class Sem:
    __slots__ = ("h", "n")

    def __init__(self, h):
        self.h = h
        self.n = 0


class Prog:
    ENGINES = ("sync", "vector", "scalar", "tensor", "gpsimd")

    def __init__(self, nc, ctx):
        self.nc = nc
        self.ctx = ctx
        self.stream = {e: [] for e in self.ENGINES}

    def sem(self, name):
        s = Sem(self.ctx.enter_context(self.nc.semaphore(name)))
        if not hasattr(self, "all_sems"):
            self.all_sems = []
        self.all_sems.append(s)
        return s

    def phase_barrier(self):
        for e in self.ENGINES:
            for s in getattr(self, "all_sems", []):
                self.wait(e, s, s.n)

    def sems(self, name, k):
        return [self.sem(f"{name}{i}") for i in range(k)]

    def wait(self, eng, sem, val):
        if val > 0:
            self.stream[eng].append(lambda e, s=sem.h, v=val: e.wait_ge(s, v))

    def barrier(self, eng, *sems):
        for s in sems:
            self.wait(eng, s, s.n)

    def op(self, eng, opname, *args, wait=(), inc=(), **kw):
        for s, v in wait:
            self.wait(eng, s, v)
        inc = tuple(inc)

        def f(e, opname=opname, args=args, kw=kw, inc=inc):
            ins = getattr(e, opname)(*args, **kw)
            for s, k in inc:
                ins = ins.then_inc(s.h, k)
            return ins

        self.stream[eng].append(f)
        ret = None
        for s, k in inc:
            s.n += k
            if ret is None:
                ret = s.n
        return ret

    def dma(self, eng, sem, pairs, wait=()):
        for j, (o, i_) in enumerate(pairs):
            self.op(eng, "dma_start", out=o, in_=i_,
                    wait=(wait if j == 0 else ()), inc=[(sem, 16)])
        return (sem, sem.n)

    def build(self, block):
        for eng in self.ENGINES:
            fns = self.stream[eng]
            if not fns:
                continue
            getattr(block, eng)(lambda e, fns=fns: [f(e) for f in fns])


def emit_ident(p, ident, sg):
    v = p.op("gpsimd", "memset", ident[:], 0.0, inc=[(sg, 1)])
    return p.op("gpsimd", "affine_select", out=ident[:], in_=ident[:],
                compare_op=ALU.not_equal, fill=1.0, base=0,
                pattern=[[-1, 128]], channel_multiplier=1,
                wait=[(sg, v)], inc=[(sg, 1)])


def alloc_ln_stats(nc, ctx, n, scr=True, pre=""):
    st = {}
    for name in ("s1", "ssq", "mean", "msq", "var", "std", "rstd", "nmr"):
        st[name] = ctx.enter_context(nc.sbuf_tensor(pre + "ln_" + name, [128, n], F32))
    if scr:
        st["scr"] = ctx.enter_context(nc.sbuf_tensor(pre + "ln_scr", [128, D], F32))
    st["eps"] = ctx.enter_context(nc.sbuf_tensor(pre + "ln_eps", [128, 1], F32))
    return st


def emit_ln_stats_finish(p, st, n, sv, ss, sq_done=None, central=True):
    w = [sq_done] if sq_done else []
    if not central:
        p.barrier("vector", sv)
        v = p.op("vector", "tensor_mul", st["msq"][:, 0:n], st["mean"][:, 0:n],
                 st["mean"][:, 0:n], inc=[(sv, 1)])
        p.wait("vector", sv, v)
        v2 = p.op("vector", "scalar_tensor_tensor", st["var"][:, 0:n],
                  st["ssq"][:, 0:n], 1.0 / D, st["msq"][:, 0:n],
                  op0=ALU.mult, op1=ALU.subtract, wait=w, inc=[(sv, 1)])
        w = [(sv, v2)]
        src = st["var"]
        sc = 1.0
    else:
        src = st["ssq"]
        sc = 1.0 / D
    s_std = p.op("scalar", "activation", out=st["std"][:, 0:n], in_=src[:, 0:n],
                 func=AF.Sqrt, bias=st["eps"][:, 0:1], scale=sc,
                 wait=w, inc=[(ss, 1)])
    v = p.op("vector", "reciprocal", st["rstd"][:, 0:n], st["std"][:, 0:n],
             wait=[(ss, s_std)], inc=[(sv, 1)])
    p.wait("vector", sv, v)
    v6 = p.op("vector", "scalar_tensor_tensor", st["nmr"][:, 0:n], st["mean"][:, 0:n],
              -1.0, st["rstd"][:, 0:n], op0=ALU.mult, op1=ALU.mult, inc=[(sv, 1)])
    return v6


def emit_transpose8(p, dst3, col0, src, ident, ps_tr, sm, sv, wait_first=()):
    last = 0
    for half in range(2):
        pt = ps_tr[half]
        tm = None
        for j in range(4):
            f = half * 4 + j
            w = list(wait_first) if (half == 0 and j == 0) else []
            tm = p.op("tensor", "transpose", pt[:, 128 * j:128 * (j + 1)],
                      src[:, 128 * f:128 * (f + 1)], ident[:], wait=w,
                      inc=([(sm, 1)] if j == 3 else []))
        last = p.op("vector", "tensor_copy",
                    dst3[:, 4 * half:4 * half + 4, col0:col0 + 128],
                    pt[:, 0:512].rearrange("p (f q) -> p f q", f=4),
                    wait=[(sm, tm)], inc=[(sv, 1)])
    return last


L0PRE_IO = dict(x_own=([TOK + 1, D], F32), w_qkv=([D, 3 * D], F32),
                b_qkv=([1, 3 * D], F32), rope_cos=([TOK, 512], F32),
                rope_sin=([TOK, 512], F32))
L0PRE_OUT = dict(xnt=([D, TOK + 1], BF16), qt=([D, TOK], BF16),
                 kt=([D, TOK], BF16), v=([TOK, D], BF16))


def emit_l0pre(nc, p, ctx, io, pre="", hook=None):
    x_own, w_qkv, b_qkv = io["x_own"], io["w_qkv"], io["b_qkv"]
    rope_cos, rope_sin = io["rope_cos"], io["rope_sin"]
    xnt_o, qt_o, kt_o, v_o = io["xnt"], io["qt"], io["kt"], io["v"]
    if True:
        sb = lambda name, shape, dt=F32: ctx.enter_context(nc.sbuf_tensor(pre + name, shape, dt))
        psa = lambda name, shape, dt=F32: ctx.enter_context(nc.psum_tensor(pre + name, shape, dt))
        x_s = [sb(f"x{i}", [128, D]) for i in range(3)]
        xb_s = sb("xb", [1, D])
        xn_s = [sb(f"xn{i}", [128, D], BF16) for i in range(2)]
        xnb_s = sb("xnb", [1, D], BF16)
        xnt = sb("xnt_s", [128, NF, TOK + 1], BF16)
        qT = sb("qT_s", [128, NF, TOK], BF16)
        kT = sb("kT_s", [128, NF, TOK], BF16)
        wst = [sb(f"wst{i}", [128, D]) for i in range(2)]
        wph = sb("wph", [128, NF, D], BF16)
        bias_f = sb("bias_f", [1, 3 * D])
        bias_h = sb("bias_h", [1, 3 * D], BF16)
        bias_l = sb("bias_l", [1, 3 * D], BF16)
        one_r = sb("one_r", [1, 128], BF16)
        qf = sb("qf", [128, D])
        qr = [sb(f"qr{i}", [128, D], BF16) for i in range(2)]
        vt_b = sb("vt0", [128, D], BF16)
        cosb = sb("cosb0", [128, 512])
        sinb = sb("sinb0", [128, 512])
        t1m, t2m = sb("t1m", [128, 512]), sb("t2m", [128, 512])
        t3m, t4m = sb("t3m", [128, 512]), sb("t4m", [128, 512])
        ident_bf = sb("ident_bf", [128, 128], BF16)
        st = alloc_ln_stats(nc, ctx, NPT + 1, scr=False, pre=pre)
        st["scr"] = qf
        ps_mm = [psa(f"psmm{i}", [128, 512]) for i in range(3)]
        ps_tr = [psa(f"pstr{i}", [128, 1024], BF16) for i in range(2)]

        sv, ss, sg, sm = (p.sem(pre + n) for n in ("sv", "ss", "sg", "sm"))
        lx = p.sems(pre + "lx", 3)
        lmisc = p.sem(pre + "lmisc")
        lw = p.sems(pre + "lw", 2)
        lcs = p.sem(pre + "lcs")
        s_v = p.sem(pre + "s_v")
        s_fin = p.sem(pre + "s_fin")
        g_id = emit_ident(p, ident_bf, sg)

        w_xb = p.dma("sync", lmisc, [(xb_s[:], x_own[0:1, :]), (bias_f[:], b_qkv)])
        p.op("vector", "memset", st["s1"][:], 0.0)
        p.op("vector", "memset", st["ssq"][:], 0.0)
        p.op("vector", "memset", one_r[:], 1.0)
        v0 = p.op("vector", "memset", st["eps"][:], 1e-5, inc=[(sv, 1)])
        p.wait("vector", sv, v0)
        v = p.op("vector", "tensor_copy", bias_h[:], bias_f[:], wait=[w_xb],
                 inc=[(sv, 1)])
        v_bias = 0
        for ch in range(3):
            p.wait("vector", sv, v if ch == 0 else v_bias)
            v2 = p.op("vector", "tensor_sub", qf[0:1, :],
                      bias_f[0:1, D * ch:D * (ch + 1)],
                      bias_h[0:1, D * ch:D * (ch + 1)], inc=[(sv, 1)])
            p.wait("vector", sv, v2)
            v_bias = p.op("vector", "tensor_copy", bias_l[0:1, D * ch:D * (ch + 1)],
                          qf[0:1, :], inc=[(sv, 1)])

        red_v = [0] * NPT
        sq_v = [0] * NPT
        for i in range(NPT):
            w = [(sv, red_v[i - 3]), (ss, sq_v[i - 3])] if i >= 3 else []
            wl = p.dma("sync", lx[i % 3],
                       [(x_s[i % 3][:], x_own[1 + 128 * i: 1 + 128 * (i + 1), :])],
                       wait=w)
            red_v[i] = p.op("vector", "reduce_sum", st["s1"][:, i:i + 1],
                            x_s[i % 3][:], axis=AX.X, wait=[wl], inc=[(sv, 1)])
            sq_v[i] = p.op("scalar", "activation", out=st["scr"][:],
                           in_=x_s[i % 3][:], func=AF.Square, bias=0.0, scale=1.0,
                           accum_out=st["ssq"][:, i:i + 1],
                           wait=[wl] + ([(ss, sq_v[i - 1])] if i >= 1
                                        else [(sv, v_bias)]),
                           inc=[(ss, 1)])
        p.op("vector", "reduce_sum", st["s1"][0:1, NPT:NPT + 1], xb_s[:], axis=AX.X,
             wait=[w_xb], inc=[(sv, 1)])
        sqb = p.op("scalar", "activation", out=st["scr"][0:1, :], in_=xb_s[:],
                   func=AF.Square, bias=0.0, scale=1.0,
                   accum_out=st["ssq"][0:1, NPT:NPT + 1],
                   wait=[(ss, sq_v[-1])], inc=[(ss, 1)])
        p.barrier("vector", sv)
        v_mean = p.op("vector", "tensor_scalar_mul", st["mean"][:, 0:NPT + 1],
                      st["s1"][:, 0:NPT + 1], 1.0 / D, inc=[(sv, 1)])
        p.wait("vector", sv, v_mean)
        v_ln = emit_ln_stats_finish(p, st, NPT + 1, sv, ss, sq_done=(ss, sqb),
                                    central=False)

        apply_v = [0] * NPT
        evac_v = [0] * NPT
        for i in range(NPT):
            j = NPT - 1 - ((NPT - 1 - i) % 3)
            w = ([(ss, apply_v[i - 3])] if i >= 3
                 else [(ss, sq_v[j]), (sv, red_v[j])])
            wl = p.dma("sync", lx[i % 3],
                       [(x_s[i % 3][:], x_own[1 + 128 * i: 1 + 128 * (i + 1), :])],
                       wait=w)
            w2 = [wl]
            if i == 0:
                w2.append((sv, v_ln))
            if i >= 2:
                w2.append((sv, evac_v[i - 2]))
            apply_v[i] = p.op("scalar", "activation", out=xn_s[i % 2][:], in_=x_s[i % 3][:],
                              func=AF.Identity, bias=st["nmr"][:, i:i + 1],
                              scale=st["rstd"][:, i:i + 1], wait=w2, inc=[(ss, 1)])
            wf = [(ss, apply_v[i]), (sg, g_id)]
            if i >= 1:
                wf.append((sv, evac_v[i - 1]))
            evac_v[i] = emit_transpose8(p, xnt, 1 + 128 * i, xn_s[i % 2],
                                        ident_bf, ps_tr, sm, sv, wait_first=wf)
        s_b = p.op("scalar", "activation", out=xnb_s[:], in_=xb_s[:],
                   func=AF.Identity, bias=st["nmr"][0:1, NPT:NPT + 1],
                   scale=st["rstd"][0:1, NPT:NPT + 1], wait=[(sv, v_ln)], inc=[(ss, 1)])
        for f in range(NF):
            w = [(ss, s_b), (sv, evac_v[NPT - 1])] if f == 0 else []
            tmb = p.op("tensor", "transpose", ps_tr[0][:, 2 * f:2 * f + 1],
                       xnb_s[0:1, 128 * f:128 * (f + 1)], ident_bf[0:1, 0:1],
                       wait=w, inc=([(sm, 1)] if f == NF - 1 else []))
        xnt_done = p.op("vector", "tensor_copy", xnt[:, :, 0:1],
                        ps_tr[0][:, 0:2 * NF].rearrange("p (f q) -> p f q", q=2)[:, :, 0:1],
                        wait=[(sm, tmb)], inc=[(sv, 1)])

        p.wait("gpsimd", sv, xnt_done)
        p.dma("gpsimd", s_fin,
              [(xnt_o[128 * f:128 * (f + 1), :], xnt[:, f, :]) for f in range(NF)])

        ev_grp = []
        rope_use = 0
        vst_v = 0
        kt_st = 0
        pend = None
        tr8_v = [0, 0]
        tr8_last = [0]

        def flush_pend():
            nonlocal pend
            if pend is None:
                return 0
            dstT_, i_, buf_, va_ = pend
            pend = None
            wf = [(sv, va_)]
            if tr8_last[0]:
                wf.append((sv, tr8_last[0]))
            tr8_v[buf_] = emit_transpose8(p, dstT_, 128 * i_, qr[buf_],
                                          ident_bf, ps_tr, sm, sv,
                                          wait_first=wf)
            tr8_last[0] = tr8_v[buf_]
            return tr8_v[buf_]

        for ph in (1, 2, 0):
            wld_v = []
            for f in range(NF):
                w = []
                if f >= 2:
                    w.append((sv, wld_v[f - 2]))
                if f == 0 and ev_grp:
                    w.append((sv, ev_grp[-1]))
                wl = p.dma("sync", lw[f % 2],
                           [(wst[f % 2][:], w_qkv[128 * f:128 * (f + 1),
                                                  D * ph: D * (ph + 1)])], wait=w)
                w2 = [wl]
                if f == 0 and ev_grp:
                    w2.append((sv, ev_grp[-1]))
                cv = p.op("vector", "tensor_copy", wph[:, f, :], wst[f % 2][:],
                          wait=w2, inc=[(sv, 1)])
                wld_v.append(cv)
            w_ready = wld_v[-1]
            for i in range(NPT):
                if ph < 2:
                    w = [(sv, rope_use)] if rope_use else []
                    w_cs = p.dma("sync", lcs,
                                 [(cosb[:], rope_cos[128 * i:128 * (i + 1), :]),
                                  (sinb[:], rope_sin[128 * i:128 * (i + 1), :])],
                                 wait=w)
                elif ph == 2 and i == 0:
                    flush_pend()
                ev_q = [0, 0]
                for ncol in range(2):
                    g = len(ev_grp)
                    pmm = ps_mm[g % 3]
                    for f in range(NF):
                        w = []
                        if f == 0:
                            w = ([(sv, max(w_ready, xnt_done, v_bias))] if g < 3
                                 else [(sv, ev_grp[g - 3]), (sv, w_ready)])
                        p.op("tensor", "matmul", pmm[:],
                             xnt[:, f, 1 + 128 * i: 1 + 128 * (i + 1)],
                             wph[:, f, 512 * ncol: 512 * (ncol + 1)],
                             start=(f == 0), stop=False, wait=w)
                    nc0 = D * ph + 512 * ncol
                    p.op("tensor", "matmul", pmm[:], one_r[0:1, :],
                         bias_h[0:1, nc0:nc0 + 512], start=False, stop=False)
                    tb = p.op("tensor", "matmul", pmm[:], one_r[0:1, :],
                              bias_l[0:1, nc0:nc0 + 512], start=False, stop=True,
                              inc=[(sm, 1)])
                    if ph == 2:
                        w = [(sm, tb)]
                        if i >= 1:
                            w.append((s_v, vst_v))
                        ev = p.op("vector", "tensor_copy",
                                  vt_b[:, 512 * ncol:512 * (ncol + 1)],
                                  pmm[:], wait=w, inc=[(sv, 1)])
                    else:
                        w = [(sm, tb)]
                        if ncol == 0 and rope_use:
                            w.append((sv, rope_use))
                        if not ev_grp:
                            w.append((ss, ss.n))
                        ev = p.op("vector", "tensor_copy",
                                  qf[:, 512 * ncol:512 * (ncol + 1)],
                                  pmm[:], wait=w, inc=[(sv, 1)])
                    ev_grp.append(ev)
                    ev_q[ncol] = ev
                if ph == 2:
                    p.wait("gpsimd", sv, ev_q[1])
                    _, vst_v = p.dma("gpsimd", s_v,
                                     [(v_o[128 * i:128 * (i + 1), :], vt_b[:])])
                    if ph == 2 and i == NPT - 1 and hook is not None:
                        hook(p, [(s_fin, kt_st), (s_v, s_v.n)])
                else:
                    buf = i % 2
                    q3 = qf[:].rearrange("p (h d) -> p h d", h=H)
                    r3 = qr[buf][:].rearrange("p (h d) -> p h d", h=H)
                    c3 = cosb[:].rearrange("p (h j) -> p h j", h=H)
                    s3 = sinb[:].rearrange("p (h j) -> p h j", h=H)
                    t13 = t1m[:].rearrange("p (h j) -> p h j", h=H)
                    t23 = t2m[:].rearrange("p (h j) -> p h j", h=H)
                    t33 = t3m[:].rearrange("p (h j) -> p h j", h=H)
                    t43 = t4m[:].rearrange("p (h j) -> p h j", h=H)
                    p.wait("vector", w_cs[0], w_cs[1])
                    p.wait("vector", sv, ev_q[1])
                    if tr8_v[buf]:
                        p.wait("vector", sv, tr8_v[buf])
                    p.op("vector", "tensor_mul", t13, q3[:, :, 0:32], c3)
                    p.op("vector", "tensor_mul", t23, q3[:, :, 32:64], s3)
                    p.op("vector", "tensor_mul", t33, q3[:, :, 0:32], s3)
                    vm = p.op("vector", "tensor_mul", t43, q3[:, :, 32:64], c3,
                              inc=[(sv, 1)])
                    p.wait("vector", sv, vm)
                    p.op("vector", "tensor_sub", r3[:, :, 0:32], t13, t23)
                    va = p.op("vector", "tensor_add", r3[:, :, 32:64], t33, t43,
                              inc=[(sv, 1)])
                    rope_use = va
                    flush_pend()
                    dstT = qT if ph == 0 else kT
                    pend = (dstT, i, buf, va)
                    if ph == 1 and i == NPT - 1:
                        kt_done = flush_pend()
                        p.wait("gpsimd", sv, kt_done)
                        _, kt_st = p.dma(
                            "gpsimd", s_fin,
                            [(kt_o[128 * f:128 * (f + 1), :], kT[:, f, :])
                             for f in range(NF)])
        flush_pend()
        p.barrier("gpsimd", sv)
        p.dma("gpsimd", s_fin,
              [(qt_o[128 * f:128 * (f + 1), :], qT[:, f, :]) for f in range(NF)])
        p.barrier("gpsimd", s_fin)
        p.barrier("gpsimd", s_v)


def emit_gather_direct(nc, p, pairs, pre=""):
    cc = p.sem(pre + "cc")
    c0 = 0
    for in_ap, out_ap in pairs:
        c0 = p.op("gpsimd", "collective_compute", "AllGather", ALU.bypass,
                  replica_groups=GROUPS, ins=[in_ap.opt()], outs=[out_ap.opt()],
                  inc=[(cc, 1)])
    p.wait("gpsimd", cc, c0)
    return (cc, c0)


def build_gather(tensors):
    nc = bass.Bass("TRN2", target_bir_lowering=False)
    with contextlib.ExitStack() as ctx:
        p = Prog(nc, ctx)
        cc = p.sem("cc")
        dms = p.sems("dm", len(tensors) * 2)
        i = 0
        for name, shape in tensors:
            inp = nc.dram_tensor(name, list(shape), BF16, kind="ExternalInput").ap()
            out = nc.dram_tensor(name + "_g", [4 * shape[0], shape[1]], BF16,
                                 kind="ExternalOutput").ap()
            bi = nc.dram_tensor(name + "_bi", list(shape), BF16, kind="Internal")
            bo = nc.dram_tensor(name + "_bo", [4 * shape[0], shape[1]], BF16,
                                kind="Internal")
            wl = p.dma("gpsimd", dms[2 * i], [(bi.ap(), inp)])
            c0 = p.op("gpsimd", "collective_compute", "AllGather", ALU.bypass,
                      replica_groups=GROUPS, ins=[bi.ap().opt()], outs=[bo.ap().opt()],
                      wait=[wl], inc=[(cc, 1)])
            p.dma("gpsimd", dms[2 * i + 1], [(out, bo.ap())], wait=[(cc, c0)])
            i += 1
        for s in dms:
            p.barrier("gpsimd", s)
        with nc.Block() as block:
            p.build(block)
    return nc


L0ATTN_IO = dict(qt=([D, TOK], BF16), kt_g=([4 * D, TOK], BF16),
                 v_g=([4 * TOK, D], BF16), w_out=([D, D], F32),
                 b_out=([128, NF], F32))
L0ATTN_OUT = dict(st=([D, TOK], F32))


def emit_l0attn(nc, p, ctx, io, pre=""):
    qt_i, ktg, vg = io["qt"], io["kt_g"], io["v_g"]
    w_out, b_out, st_o = io["w_out"], io["b_out"], io["st"]
    if True:
        sb = lambda name, shape, dt=F32: ctx.enter_context(nc.sbuf_tensor(pre + name, shape, dt))
        psa = lambda name, shape, dt=F32: ctx.enter_context(nc.psum_tensor(pre + name, shape, dt))
        qT = sb("qT", [128, NF, TOK], BF16)
        OT = sb("OT", [128, NF, TOK], BF16)
        ktb = [sb(f"ktb{i}", [128, 4, TOK], BF16) for i in range(2)]
        vraw = sb("vraw", [128, KT, 128], BF16)
        vt65 = [sb(f"vt65_{i}", [128, KT, 130], BF16) for i in range(2)]
        pT = [sb(f"pT{i}", [128, 1024], BF16) for i in range(2)]
        wo_bf = sb("wo_bf", [128, NF, D], BF16)
        wst = [sb(f"wst{i}", [128, D]) for i in range(2)]
        bo_s = sb("bo_s", [128, NF])
        ones_bf = sb("ones_bf", [128, 1], BF16)
        ones_r = sb("ones_r", [1, 64])
        dr = sb("dr", [1, 1024])
        bc_sb = sb("bc_sb", [64, 1024])
        ev_sb = [sb(f"ev{i}", [128, 512]) for i in range(2)]
        ps_sc = [psa(f"ps_sc{i}", [128, 1024]) for i in range(2)]
        ps_o1 = psa("ps_o1", [65, 512])
        ps_o2 = psa("ps_o2", [65, 512])
        ps_bc = psa("ps_bc", [64, 512])
        ps_pr = psa("ps_pr", [128, 512])

        sv, ss, sm = (p.sem(pre + n) for n in ("sv", "ss", "sm"))
        l_q = p.sems(pre + "l_q", 2)
        lw = p.sems(pre + "lw", 2)
        l_kt = p.sems(pre + "l_kt", 2)
        l_vt = p.sems(pre + "l_vt", 2)
        s_st = p.sems(pre + "s_st", 2)

        p.op("vector", "memset", ones_bf[:], 1.0)
        v_one = p.op("vector", "memset", ones_r[:], 1.0, inc=[(sv, 1)])

        pairs = [(qT[:, f, :], qt_i[128 * f:128 * (f + 1), :]) for f in range(NF)]
        pairs.append((bo_s[:], b_out))
        w_q = p.dma("sync", l_q[0], pairs)
        wcast_v = []
        for f in range(NF):
            w = [(sv, wcast_v[f - 2])] if f >= 2 else []
            wl = p.dma("sync", lw[f % 2],
                       [(wst[f % 2][:], w_out[128 * f:128 * (f + 1), :])], wait=w)
            cv = p.op("vector", "tensor_copy", wo_bf[:, f, :], wst[f % 2][:],
                      wait=[wl], inc=[(sv, 1)])
            wcast_v.append(cv)

        vraw_free = [0]

        def load_hp(hp, buf, wv):
            p.dma("sync", l_kt[buf],
                  [(ktb[buf][:, rk, :],
                    ktg[D * rk + 128 * hp: D * rk + 128 * hp + 128, :])
                   for rk in range(4)], wait=wv)
            wld = p.dma("gpsimd", l_vt[0],
                        [(vraw[:, 8 * blk:8 * (blk + 1), :],
                          vg[1024 * blk:1024 * (blk + 1), 128 * hp:128 * (hp + 1)]
                          .rearrange("(t p) c -> p t c", p=128))
                         for blk in range(8)],
                        wait=([(sv, vraw_free[0])] if vraw_free[0] else []))
            p.op("vector", "memset", vt65[buf][:, :, 64:65], 1.0, wait=wv)
            p.op("vector", "memset", vt65[buf][:, :, 129:130], 1.0)
            p.op("vector", "tensor_copy", vt65[buf][:, :, 0:64],
                 vraw[:, :, 0:64], wait=[wld])
            cpv = p.op("vector", "tensor_copy", vt65[buf][:, :, 65:129],
                       vraw[:, :, 64:128], inc=[(sv, 1)])
            vraw_free[0] = cpv
            return [(l_kt[buf], l_kt[buf].n), (sv, cpv)]

        hp_ld = [None] * (NF + 1)
        hp_use = [0] * (NF + 1)
        hp_ld[0] = load_hp(0, 0, [])
        norm_v = 0
        for hp in range(NF):
            buf = hp % 2
            if hp + 1 < NF:
                wv = [(sm, hp_use[hp - 1])] if hp >= 1 else []
                hp_ld[hp + 1] = load_hp(hp + 1, 1 - buf, wv)
            for c in range(4):
                exp_v = [0] * KT
                qk_v = [0] * KT
                avdone_v = [0] * KT

                def emit_qk(kt, extra):
                    par = kt % 2
                    ktap = ktb[buf][:, kt // 16,
                                    128 * (kt % 16):128 * (kt % 16) + 128]
                    p.op("tensor", "matmul", ps_sc[par][:, 0:512],
                         ktap[0:64, :], qT[0:64, hp, 512 * c:512 * (c + 1)],
                         start=True, stop=True, tile_position=(0, 0), wait=extra)
                    qk_v[kt] = p.op("tensor", "matmul", ps_sc[par][:, 512:1024],
                                    ktap[64:128, :],
                                    qT[64:128, hp, 512 * c:512 * (c + 1)],
                                    start=True, stop=True, tile_position=(64, 0),
                                    inc=[(sm, 1)])

                w0 = [(sv, v_one)]
                if c == 0:
                    w0 = hp_ld[hp] + [w_q, (sv, v_one)]
                emit_qk(0, w0)
                for kt in range(KT):
                    par = kt % 2
                    we = [(sm, qk_v[kt])]
                    if kt >= 2:
                        we.append((sm, avdone_v[kt - 2]))
                    exp_v[kt] = p.op("scalar", "activation", out=pT[par][:, :],
                                     in_=ps_sc[par][:, :], func=AF.Exp, bias=0.0,
                                     scale=L0_SCALE, wait=we, inc=[(ss, 1)])
                    if kt + 1 < KT:
                        emit_qk(kt + 1,
                                [(ss, exp_v[kt - 1])] if kt >= 1 else [])
                    w = [(ss, exp_v[kt])]
                    if kt == 0:
                        w.append((sv, norm_v))
                    p.op("tensor", "matmul", ps_o1[0:65, :],
                         vt65[buf][:, kt, 0:65], pT[par][:, 0:512],
                         start=(kt == 0), stop=(kt == KT - 1), wait=w)
                    avdone_v[kt] = p.op("tensor", "matmul", ps_o2[0:65, :],
                                        vt65[buf][:, kt, 65:130],
                                        pT[par][:, 512:1024],
                                        start=(kt == 0), stop=(kt == KT - 1),
                                        inc=[(sm, 1)])
                hp_use[hp] = avdone_v[KT - 1]
                v_r = p.op("vector", "reciprocal", dr[0:1, 0:512], ps_o1[64:65, :],
                           wait=[(sm, avdone_v[KT - 1])], inc=[(sv, 1)])
                v_r2 = p.op("vector", "reciprocal", dr[0:1, 512:1024], ps_o2[64:65, :],
                            inc=[(sv, 1)])
                t_b = p.op("tensor", "matmul", ps_bc[:], ones_r[0:1, :],
                           dr[0:1, 0:512], start=True, stop=True,
                           wait=[(sv, v_r2)], inc=[(sm, 1)])
                v_c = p.op("vector", "tensor_copy", bc_sb[:, 0:512], ps_bc[:],
                           wait=[(sm, t_b)], inc=[(sv, 1)])
                t_b2 = p.op("tensor", "matmul", ps_bc[:], ones_r[0:1, :],
                            dr[0:1, 512:1024], start=True, stop=True,
                            wait=[(sv, v_c)], inc=[(sm, 1)])
                v_c2 = p.op("vector", "tensor_copy", bc_sb[:, 512:1024], ps_bc[:],
                            wait=[(sm, t_b2)], inc=[(sv, 1)])
                p.wait("vector", sv, v_c)
                p.op("vector", "tensor_mul", OT[0:64, hp, 512 * c:512 * (c + 1)],
                     ps_o1[0:64, :], bc_sb[:, 0:512])
                p.wait("vector", sv, v_c2)
                norm_v = p.op("vector", "tensor_mul",
                              OT[64:128, hp, 512 * c:512 * (c + 1)],
                              ps_o2[0:64, :], bc_sb[:, 512:1024], inc=[(sv, 1)])
        ev_v = [0, 0]
        stv = [0, 0]
        for m in range(NF):
            for c in range(4):
                j = m * 4 + c
                for f in range(NF):
                    w = []
                    if f == 0:
                        w = [(sv, norm_v)] if j == 0 else [(sv, ev_v[(j - 1) % 2])]
                    tp = p.op("tensor", "matmul", ps_pr[:],
                              wo_bf[:, f, 128 * m:128 * (m + 1)],
                              OT[:, f, 512 * c:512 * (c + 1)],
                              start=(f == 0), stop=(f == NF - 1), wait=w,
                              inc=([(sm, 1)] if f == NF - 1 else []))
                w = [(sm, tp)]
                if j >= 2:
                    w.append((s_st[j % 2], stv[j % 2]))
                ve = p.op("vector", "tensor_scalar_add", ev_sb[j % 2][:], ps_pr[:],
                          bo_s[:, m:m + 1], wait=w, inc=[(sv, 1)])
                ev_v[j % 2] = ve
                _, stv[j % 2] = p.dma(
                    "gpsimd", s_st[j % 2],
                    [(st_o[128 * m:128 * (m + 1), 512 * c:512 * (c + 1)],
                      ev_sb[j % 2][:])], wait=[(sv, ve)])
        p.barrier("gpsimd", s_st[0], s_st[1])


L0RWKV_IO = dict(xnt=([D, TOK + 1], BF16), w_r=([D, D], F32), w_k=([D, D], F32),
                 w_v=([D, D], F32), mu_r=([128, NF], F32), mu_k=([128, NF], F32),
                 st=([D, TOK], F32), x_own=([TOK + 1, D], F32))
L0RWKV_OUT = dict(x1=([TOK, D], F32))


def emit_l0rwkv(nc, p, ctx, io, pre=""):
    xnt_i, w_r, w_k, w_v = io["xnt"], io["w_r"], io["w_k"], io["w_v"]
    mu_r, mu_k, st_i, x_own = io["mu_r"], io["mu_k"], io["st"], io["x_own"]
    x1_o = io["x1"]
    if True:
        sb = lambda name, shape, dt=F32: ctx.enter_context(nc.sbuf_tensor(pre + name, shape, dt))
        psa = lambda name, shape, dt=F32: ctx.enter_context(nc.psum_tensor(pre + name, shape, dt))
        xnt = sb("xnt_s", [128, NF, TOK + 1], BF16)
        wT = {n: sb(f"wT_{n}", [128, NF, D], BF16) for n in ("r", "k", "v")}
        wst = [sb(f"wst{i}", [128, D]) for i in range(2)]
        wbf = sb("wbf", [128, D], BF16)
        mur_s = sb("mur_s", [128, NF])
        muk_s = sb("muk_s", [128, NF])
        tdif = sb("tdif", [128, 512], BF16)
        xr = [sb(f"xr{i}", [128, NF, 512], BF16) for i in range(2)]
        xk = [sb(f"xk{i}", [128, NF, 512], BF16) for i in range(2)]
        rb = sb("rb", [128, NF, 512], BF16)
        kab = sb("kab", [128, NF, 512], BF16)
        gb = sb("gb", [128, 512], BF16)
        stc = sb("stc", [128, NF, 512])
        uT = sb("uT", [128, NF, 512])
        gf = sb("gf", [128, 512])
        x_t = [sb(f"x_t{i}", [128, D]) for i in range(2)]
        x1_t = [sb(f"x1_t{i}", [128, D]) for i in range(2)]
        ident_bf = sb("ident_bf", [128, 128], BF16)
        ident_f = sb("ident_f", [128, 128])
        ps_mm = [psa(f"psmm{i}", [128, 512]) for i in range(3)]
        ps_tr = [psa(f"pstr{i}", [128, 1024], BF16) for i in range(2)]
        ps_tf = [psa(f"pstf{i}", [128, 512]) for i in range(2)]

        sv, ss, sg, sm = (p.sem(pre + n) for n in ("sv", "ss", "sg", "sm"))
        l_xnt = p.sem(pre + "l_xnt")
        lw = p.sems(pre + "lw", 2)
        l_stc = p.sem(pre + "l_stc")
        l_x = p.sems(pre + "l_x", 2)
        s_x1 = p.sems(pre + "s_x1", 2)
        emit_ident(p, ident_bf, sg)
        g_id = emit_ident(p, ident_f, sg)

        pairs = [(xnt[:, f, :], xnt_i[128 * f:128 * (f + 1), :]) for f in range(NF)]
        pairs += [(mur_s[:], mu_r), (muk_s[:], mu_k)]
        w_xnt = p.dma("sync", l_xnt, pairs)

        tm_last = 0
        tr_last = 0
        for (n, wdram) in (("r", w_r), ("k", w_k), ("v", w_v)):
            for m in range(NF):
                w = [(sv, tr_last)] if tr_last else []
                wl = p.dma("sync", lw[m % 2],
                           [(wst[m % 2][:], wdram[128 * m:128 * (m + 1), :])], wait=w)
                cv = p.op("vector", "tensor_copy", wbf[:], wst[m % 2][:],
                          wait=[wl] + ([(sm, tm_last)] if tm_last else []),
                          inc=[(sv, 1)])
                tr_last = emit_transpose8(p, wT[n], 128 * m, wbf, ident_bf,
                                          ps_tr, sm, sv,
                                          wait_first=[(sv, cv), (sg, g_id)] +
                                          ([(sv, tr_last)] if tr_last else []))
                tm_last = sm.n
        w_ready = tr_last

        ev_grp = []
        gf_free = None
        stc_free = 0
        uT_free = 0
        stv = [0, 0]
        mix_done_v = [0] * 4
        rk_done_v = [0] * 4

        def emit_mixes(cch, extra):
            c0 = 512 * cch
            mix_prev = 0
            for f in range(NF):
                w = list(extra) if f == 0 else []
                if mix_prev:
                    w.append((sv, mix_prev))
                vt = p.op("vector", "tensor_sub", tdif[:],
                          xnt[:, f, 1 + c0:1 + c0 + 512], xnt[:, f, c0:c0 + 512],
                          wait=w, inc=[(sv, 1)])
                p.wait("vector", sv, vt)
                p.op("vector", "scalar_tensor_tensor", xr[cch % 2][:, f, :], tdif[:],
                     mur_s[:, f:f + 1], xnt[:, f, c0:c0 + 512],
                     op0=ALU.mult, op1=ALU.add)
                mix_prev = p.op("vector", "scalar_tensor_tensor", xk[cch % 2][:, f, :],
                                tdif[:], muk_s[:, f:f + 1], xnt[:, f, c0:c0 + 512],
                                op0=ALU.mult, op1=ALU.add, inc=[(sv, 1)])
            mix_done_v[cch] = mix_prev

        emit_mixes(0, [w_xnt])
        for cch in range(4):
            c0 = 512 * cch
            w_st = p.dma("sync", l_stc,
                         [(stc[:, f, :], st_i[128 * f:128 * (f + 1), c0:c0 + 512])
                          for f in range(NF)],
                         wait=([(sv, stc_free)] if cch >= 1 else []))
            mix_done = mix_done_v[cch]
            for (src, wn, kind) in ((xr[cch % 2], "r", "sig"),
                                    (xk[cch % 2], "k", "relu2")):
                for m in range(NF):
                    g = len(ev_grp)
                    pmm = ps_mm[g % 3]
                    for f in range(NF):
                        w = []
                        if f == 0:
                            w = ([(sv, max(mix_done, w_ready))] if g < 3
                                 else [(sv, ev_grp[g - 3]), (sv, mix_done)])
                        tb = p.op("tensor", "matmul", pmm[:],
                                  wT[wn][:, f, 128 * m:128 * (m + 1)],
                                  src[:, f, :], start=(f == 0),
                                  stop=(f == NF - 1), wait=w,
                                  inc=([(sm, 1)] if f == NF - 1 else []))
                    if kind == "sig":
                        ev = p.op("scalar", "activation", out=rb[:, m, :], in_=pmm[:],
                                  func=AF.Sigmoid, bias=0.0, scale=1.0,
                                  wait=[(sm, tb)], inc=[(sv, 1)])
                    else:
                        wgf = [(sm, tb)]
                        if gf_free is not None:
                            wgf.append(gf_free)
                        sr = p.op("scalar", "activation", out=gf[:], in_=pmm[:],
                                  func=AF.Relu, bias=0.0, scale=1.0,
                                  wait=wgf, inc=[(ss, 1)])
                        ev = p.op("vector", "tensor_mul", kab[:, m, :], gf[:], gf[:],
                                  wait=[(ss, sr)], inc=[(sv, 1)])
                        gf_free = (sv, ev)
                    ev_grp.append(ev)
            rk_done_v[cch] = sm.n
            if cch + 1 < 4:
                emit_mixes(cch + 1, [(sm, rk_done_v[cch])])
            kact_done = ev_grp[-1]
            u_prev = 0
            for m in range(NF):
                g = len(ev_grp)
                pmm = ps_mm[g % 3]
                for f in range(NF):
                    w = []
                    if f == 0:
                        w = ([(sv, kact_done)] if g < 3
                             else [(sv, ev_grp[g - 3]), (sv, kact_done)])
                    tb = p.op("tensor", "matmul", pmm[:],
                              wT["v"][:, f, 128 * m:128 * (m + 1)],
                              kab[:, f, :], start=(f == 0), stop=(f == NF - 1),
                              wait=w, inc=([(sm, 1)] if f == NF - 1 else []))
                vkv = p.op("vector", "tensor_mul", gf[:], pmm[:], rb[:, m, :],
                           wait=[(sm, tb), gf_free], inc=[(sv, 1)])
                ev_grp.append(vkv)
                sg_ = p.op("scalar", "activation", out=gb[:], in_=gf[:],
                           func=AF.Silu, bias=0.0, scale=1.0,
                           wait=[(sv, vkv)] + ([(sv, u_prev)] if u_prev else []),
                           inc=[(ss, 1)])
                gf_free = (ss, sg_)
                w = [(ss, sg_), w_st]
                if cch > 0 and m == 0:
                    w.append((sm, uT_free))
                u_prev = p.op("vector", "tensor_mul", uT[:, m, :], stc[:, m, :], gb[:],
                              wait=w, inc=[(sv, 1)])
            u_done = u_prev
            stc_free = u_done
            res_v = 0
            tr_done = 0
            for tb4 in range(4):
                ti = 4 * cch + tb4
                wl = p.dma("sync", l_x[tb4 % 2],
                           [(x_t[tb4 % 2][:],
                             x_own[1 + 128 * ti:1 + 128 * (ti + 1), :])],
                           wait=([(sv, res_v)] if ti >= 2 else []))
                for half in range(2):
                    pf = ps_tf[half]
                    for j in range(4):
                        m = 4 * half + j
                        w = []
                        if m == 0 and tb4 == 0:
                            w = [(sv, u_done), (sg, g_id)]
                        if j == 0 and tr_done:
                            w.append((sv, tr_done))
                        p.op("tensor", "transpose",
                             pf[:, 128 * j:128 * (j + 1)],
                             uT[:, m, 128 * tb4:128 * (tb4 + 1)], ident_f[:],
                             wait=w, inc=([(sm, 1)] if j == 3 else []))
                    wv_ = [(sm, sm.n), wl]
                    if half == 0 and ti >= 2:
                        wv_.append((s_x1[tb4 % 2], stv[tb4 % 2]))
                    tr_done = res_v = p.op(
                        "vector", "tensor_add",
                        x1_t[tb4 % 2][:, 512 * half:512 * (half + 1)],
                        x_t[tb4 % 2][:, 512 * half:512 * (half + 1)],
                        pf[:, 0:512], wait=wv_, inc=[(sv, 1)])
                p.wait("gpsimd", sv, res_v)
                _, stv[tb4 % 2] = p.dma(
                    "gpsimd", s_x1[tb4 % 2],
                    [(x1_o[128 * ti:128 * (ti + 1), :], x1_t[tb4 % 2][:])])
            uT_free = sm.n
        p.barrier("gpsimd", s_x1[0], s_x1[1])


L1PRE_IO = dict(x1=([TOK, D], F32), wq1=([D, D], F32), wk1=([D, D], F32),
                wv1=([D, D], F32), wt1=([D, D], F32), alpha1=([1, 1], F32),
                bias1=([128, NF], F32))
L1PRE_OUT = dict(qt2=([D, TOK], BF16), kt2=([D, TOK], BF16),
                 v2=([TOK, D], BF16), g2t=([D, TOK], BF16))


def emit_l1pre(nc, p, ctx, io, pre="", hook=None):
    x1_i, wq, wk, wv, wt = io["x1"], io["wq1"], io["wk1"], io["wv1"], io["wt1"]
    alpha, bias1 = io["alpha1"], io["bias1"]
    qt2_o, kt2_o, v2_o, g2t_o = io["qt2"], io["kt2"], io["v2"], io["g2t"]
    if True:
        sb = lambda name, shape, dt=F32: ctx.enter_context(nc.sbuf_tensor(pre + name, shape, dt))
        psa = lambda name, shape, dt=F32: ctx.enter_context(nc.psum_tensor(pre + name, shape, dt))
        x_b8 = sb("xb8", [128, NPT // 2, D])
        xn_s = [sb(f"xn{i}", [128, D], BF16) for i in range(2)]
        xnt = sb("xnt_s", [128, NF, TOK], BF16)
        wq_bf = sb("wq_bf", [128, NF, D], BF16)
        wk_bf = sb("wk_bf", [128, NF, D], BF16)
        wv_bf = sb("wv_bf", [128, NF, D], BF16)
        wt_f = sb("wt_f", [128, NF, D])
        ternT = sb("ternT", [128, NF, D], BF16)
        tern_b = sb("tern_b", [128, D], BF16)
        ta_s = sb("ta_s", [128, D])
        tb_s = sb("tb_s", [128, D])
        wst = [sb(f"wst{i}", [128, D]) for i in range(2)]
        asum = sb("asum", [128, NF])
        atot = sb("atot", [128, 1])
        ones_c = sb("ones_c", [128, 1])
        ones_row = sb("ones_row", [1, 128])
        dcol = sb("dcol", [128, 1])
        ndcol = sb("ndcol", [128, 1])
        acol = sb("acol", [128, 1])
        tot1 = sb("tot1", [1, 1])
        al1 = sb("al1", [1, 1])
        bi_s = sb("bi_s", [128, NF])
        gf = sb("gf", [128, 512])
        evb = [sb(f"evb{i}", [128, 512], BF16) for i in range(2)]
        ident_bf = sb("ident_bf", [128, 128], BF16)
        st = alloc_ln_stats(nc, ctx, NPT, pre=pre)
        ps_mm = [psa(f"psmm{i}", [128, 512]) for i in range(3)]
        ps_tr = [psa(f"pstr{i}", [128, 1024], BF16) for i in range(2)]
        ps_sc1 = psa("ps_sc1", [1, 512])
        ps_bc1 = psa("ps_bc1", [128, 512])

        sv, ss, sg, sm = (p.sem(pre + n) for n in ("sv", "ss", "sg", "sm"))
        lx = p.sems(pre + "lx", 4)
        lmisc = p.sem(pre + "lmisc")
        lw = p.sems(pre + "lw", 2)
        s_ev = p.sems(pre + "s_ev", 2)
        g_id = emit_ident(p, ident_bf, sg)

        p.op("vector", "memset", st["s1"][:], 0.0)
        p.op("vector", "memset", st["ssq"][:], 0.0)
        p.op("vector", "memset", ones_c[:], 1.0)
        p.op("vector", "memset", ones_row[:], 1.0)
        v0 = p.op("vector", "memset", st["eps"][:], 1e-5, inc=[(sv, 1)])
        p.wait("vector", sv, v0)
        w_misc = p.dma("sync", lmisc, [(bi_s[:], bias1), (al1[:], alpha)])

        wc_v = []
        for (dst, src) in ((wq_bf, wq), (wk_bf, wk), (wv_bf, wv), (wt_f, wt)):
            for f in range(NF):
                w = [(sv, wc_v[-2])] if len(wc_v) >= 2 else []
                wl = p.dma("sync", lw[f % 2],
                           [(wst[f % 2][:], src[128 * f:128 * (f + 1), :])], wait=w)
                cv = p.op("vector", "tensor_copy", dst[:, f, :], wst[f % 2][:],
                          wait=[wl], inc=[(sv, 1)])
                wc_v.append(cv)
        w_ready = wc_v[-1]

        p.wait("vector", sv, w_ready)
        for f in range(NF):
            p.op("vector", "tensor_reduce", asum[:, f:f + 1], wt_f[:, f, :],
                 axis=AX.X, op=ALU.add, apply_absolute_value=True)
        va = p.op("vector", "memset", atot[:], 0.0, inc=[(sv, 1)])
        p.wait("vector", sv, va)
        vaa = p.op("vector", "tensor_reduce", atot[:, 0:1], asum[:, 0:NF],
                   axis=AX.X, op=ALU.add, inc=[(sv, 1)])
        tsum = p.op("tensor", "matmul", ps_sc1[:, 0:1], atot[:, 0:1], ones_c[:, 0:1],
                    start=True, stop=True, wait=[(sv, vaa)], inc=[(sm, 1)])
        vts = p.op("vector", "tensor_copy", tot1[:], ps_sc1[:, 0:1], wait=[(sm, tsum)],
                   inc=[(sv, 1)])
        p.wait("vector", sv, vts)
        vsc = p.op("vector", "tensor_scalar_mul", tot1[:], tot1[:], 0.7 / (D * D),
                   inc=[(sv, 1)])
        tb1 = p.op("tensor", "matmul", ps_bc1[:, 0:1], ones_row[0:1, :], tot1[0:1, 0:1],
                   start=True, stop=True, wait=[(sv, vsc)], inc=[(sm, 1)])
        vd = p.op("vector", "tensor_copy", dcol[:], ps_bc1[:, 0:1], wait=[(sm, tb1)],
                  inc=[(sv, 1)])
        p.wait("vector", sv, vd)
        vnd = p.op("vector", "tensor_scalar_mul", ndcol[:], dcol[:], -1.0, inc=[(sv, 1)])
        tb2 = p.op("tensor", "matmul", ps_bc1[:, 0:1], ones_row[0:1, :], al1[0:1, 0:1],
                   start=True, stop=True, wait=[(sv, vd), w_misc], inc=[(sm, 1)])
        vac = p.op("vector", "tensor_copy", acol[:], ps_bc1[:, 0:1], wait=[(sm, tb2)],
                   inc=[(sv, 1)])

        tr_last = 0
        for f in range(NF):
            w = [(sv, vnd)] if f == 0 else [(sv, tr_last)]
            p.op("vector", "tensor_single_scalar", ta_s[:], wt_f[:, f, :],
                 dcol[:, 0:1], ALU.is_gt, wait=w)
            vg_ = p.op("vector", "tensor_single_scalar", tb_s[:], wt_f[:, f, :],
                       ndcol[:, 0:1], ALU.is_lt, inc=[(sv, 1)])
            p.wait("vector", sv, vg_)
            vt_ = p.op("vector", "tensor_sub", tern_b[:], ta_s[:], tb_s[:], inc=[(sv, 1)])
            tr_last = emit_transpose8(p, ternT, 128 * f, tern_b, ident_bf,
                                      ps_tr, sm, sv,
                                      wait_first=[(sv, vt_), (sg, g_id)] +
                                      ([(sv, tr_last)] if tr_last else []))
        tern_ready = tr_last

        xtile = lambda i: wt_f[:, i, :] if i < 8 else x_b8[:, i - 8, :]
        wg = [0] * 4
        for g in range(4):
            src = x1_i[512 * g: 512 * (g + 1), :]
            w = [(sv, tern_ready)] if g < 2 else []
            wg[g] = p.dma("sync", lx[g],
                          [((wt_f[:, 4 * g:4 * (g + 1), :] if g < 2
                             else x_b8[:, 4 * (g - 2):4 * (g - 1), :]),
                            src.rearrange("(t p) d -> p t d", p=128))], wait=w)
        order = list(range(8, NPT)) + list(range(8))
        red_v = [0] * NPT
        sq_v = [0] * NPT
        for pos, i in enumerate(order):
            wl = wg[i // 4]
            red_v[pos] = p.op("vector", "reduce_sum", st["s1"][:, i:i + 1],
                              xtile(i), axis=AX.X, wait=[wl], inc=[(sv, 1)])
            sq_v[pos] = p.op("scalar", "activation", out=st["scr"][:],
                             in_=xtile(i), func=AF.Square, bias=0.0, scale=1.0,
                             accum_out=st["ssq"][:, i:i + 1],
                             wait=[wl] + ([(ss, sq_v[pos - 1])] if pos >= 1 else []),
                             inc=[(ss, 1)])
        p.barrier("vector", sv)
        v_mean = p.op("vector", "tensor_scalar_mul", st["mean"][:, 0:NPT],
                      st["s1"][:, 0:NPT], 1.0 / D, inc=[(sv, 1)])
        p.wait("vector", sv, v_mean)
        v_ln = emit_ln_stats_finish(p, st, NPT, sv, ss, sq_done=(ss, sq_v[-1]),
                                    central=False)
        apply_v = [0] * NPT
        evac_v = [0] * NPT
        for pos, i in enumerate(order):
            w2 = []
            if pos == 0:
                w2.append((sv, v_ln))
            if pos >= 2:
                w2.append((sv, evac_v[pos - 2]))
            apply_v[pos] = p.op("scalar", "activation", out=xn_s[pos % 2][:],
                                in_=xtile(i),
                                func=AF.Identity, bias=st["nmr"][:, i:i + 1],
                                scale=st["rstd"][:, i:i + 1], wait=w2, inc=[(ss, 1)])
            wf = [(ss, apply_v[pos]), (sg, g_id)]
            if pos >= 1:
                wf.append((sv, evac_v[pos - 1]))
            evac_v[pos] = emit_transpose8(p, xnt, 128 * i, xn_s[pos % 2],
                                          ident_bf, ps_tr, sm, sv, wait_first=wf)
        xnt_done = evac_v[-1]


        ev_grp = []
        stv = [0, 0]
        cnt = [0]

        def proj_fm(lhs3, dma_out, ready, epilogue):
            for m in range(NF):
                for cch in range(4):
                    g = len(ev_grp)
                    pmm = ps_mm[g % 3]
                    for f in range(NF):
                        w = []
                        if f == 0:
                            w = ([(sv, ready)] if g < 3
                                 else [(sv, ev_grp[g - 3]), (sv, ready)])
                        tb = p.op("tensor", "matmul", pmm[:],
                                  lhs3[:, f, 128 * m:128 * (m + 1)],
                                  xnt[:, f, 512 * cch:512 * (cch + 1)],
                                  start=(f == 0), stop=(f == NF - 1), wait=w,
                                  inc=([(sm, 1)] if f == NF - 1 else []))
                    j = cnt[0]
                    wv_ = [(sm, tb)]
                    if j >= 2:
                        wv_.append((s_ev[j % 2], stv[j % 2]))
                    esem, ev = epilogue(pmm, m, j % 2, wv_)
                    ev_grp.append(ev if esem is sv else sv.n)
                    _, stv[j % 2] = p.dma(
                        "gpsimd", s_ev[j % 2],
                        [(dma_out[128 * m:128 * (m + 1), 512 * cch:512 * (cch + 1)],
                          evb[j % 2][:])], wait=[(esem, ev)])
                    cnt[0] += 1

        def copy_ep(pmm, m, par, wv_):
            return sv, p.op("vector", "tensor_copy", evb[par][:], pmm[:], wait=wv_,
                            inc=[(sv, 1)])

        gf_free = [0]

        def g2_ep(pmm, m, par, wv_):
            if gf_free[0]:
                wv_ = wv_ + [(ss, gf_free[0])]
            vv = p.op("vector", "tensor_scalar", gf[:], pmm[:], acol[:, 0:1],
                      bi_s[:, m:m + 1], op0=ALU.mult, op1=ALU.add,
                      wait=wv_ + [(sv, vac)], inc=[(sv, 1)])
            r = p.op("scalar", "activation", out=evb[par][:], in_=gf[:],
                     func=AF.Silu, bias=0.0, scale=1.0, wait=[(sv, vv)],
                     inc=[(ss, 1)])
            gf_free[0] = r
            return ss, r

        proj_fm(wk_bf, kt2_o, max(xnt_done, w_ready), copy_ep)
        for ti in range(NPT):
            for dv in range(2):
                g = len(ev_grp)
                pmm = ps_mm[g % 3]
                for f in range(NF):
                    w = []
                    if f == 0:
                        w = [(sv, ev_grp[g - 3])] if g >= 3 else []
                    tb = p.op("tensor", "matmul", pmm[:],
                              xnt[:, f, 128 * ti:128 * (ti + 1)],
                              wv_bf[:, f, 512 * dv:512 * (dv + 1)],
                              start=(f == 0), stop=(f == NF - 1), wait=w,
                              inc=([(sm, 1)] if f == NF - 1 else []))
                j = cnt[0]
                wv_ = [(sm, tb)]
                if j >= 2:
                    wv_.append((s_ev[j % 2], stv[j % 2]))
                ev = p.op("vector", "tensor_copy", evb[j % 2][:], pmm[:], wait=wv_,
                          inc=[(sv, 1)])
                ev_grp.append(ev)
                _, stv[j % 2] = p.dma(
                    "gpsimd", s_ev[j % 2],
                    [(v2_o[128 * ti:128 * (ti + 1), 512 * dv:512 * (dv + 1)],
                      evb[j % 2][:])], wait=[(sv, ev)])
                cnt[0] += 1
        if hook is not None:
            hook(p, [(s_ev[0], s_ev[0].n), (s_ev[1], s_ev[1].n)])
        proj_fm(wq_bf, qt2_o, max(xnt_done, w_ready), copy_ep)
        proj_fm(ternT, g2t_o, tern_ready, g2_ep)

        p.barrier("gpsimd", s_ev[0], s_ev[1])


def _ktg_slice(ktg, f, ktblk):
    rk = ktblk // 2
    return ktg[D * rk + 128 * f: D * rk + 128 * (f + 1),
               1024 * (ktblk % 2):1024 * (ktblk % 2) + 1024]


L1ATTN_IO = dict(qt2=([D, TOK], BF16), kt2_g=([4 * D, TOK], BF16),
                 v2_g=([4 * TOK, D], BF16), wo1=([D, D], F32),
                 g2t=([D, TOK], BF16), x1=([TOK, D], F32),
                 mask=([16, 8, 128, 8, 128], BF16))
L1ATTN_OUT = dict(out=([TOK, D], F32))


def emit_l1attn(nc, p, ctx, io, pre=""):
    qt2, ktg, vg, wo = io["qt2"], io["kt2_g"], io["v2_g"], io["wo1"]
    g2t, x1_i, mask, out_o = io["g2t"], io["x1"], io["mask"], io["out"]
    if True:
        sb = lambda name, shape, dt=F32: ctx.enter_context(nc.sbuf_tensor(pre + name, shape, dt))
        psa = lambda name, shape, dt=F32: ctx.enter_context(nc.psum_tensor(pre + name, shape, dt))
        qiq = [sb(f"qiq{i}", [128, NF, 512], BF16) for i in range(2)]
        ktb = [sb(f"ktb{i}", [128, NF, 1024], BF16) for i in range(2)]
        vtb = [sb(f"vtb{i}", [128, 8, 512], BF16) for i in range(2)]
        mkb = sb("mkb", [128, 4, 8, 128], BF16)
        PT = sb("PT", [128, KT, 512], BF16)
        pq = sb("pq", [128, 512], BF16)
        wo_bf = sb("wo_bf", [128, NF, D], BF16)
        wst = [sb("wst0", [128, D])]
        ones_bf = sb("ones_bf", [128, 1], BF16)
        dsb = sb("dsb", [128, 4])
        drc = sb("drc", [128, 4])
        o_bf3 = sb("o_bf3", [128, 4, D], BF16)
        oT = sb("oT", [128, NF, 512], BF16)
        g2c = sb("g2c", [128, NF, 512], BF16)
        s2c = sb("s2c", [128, 512])
        uT = sb("uT", [128, NF, 512])
        x_t = [sb("x_t0", [128, D])]
        xo_t = [sb("xo_t0", [128, D])]
        ident_bf = sb("ident_bf", [128, 128], BF16)
        ident_f = sb("ident_f", [128, 128])
        ps_sc = [psa(f"ps_sc{i}", [128, 512]) for i in range(2)]
        ps_o = [psa(f"ps_o{i}", [128, 512]) for i in range(4)]
        ps_tb = [psa(f"ps_tb{i}", [128, 1024], BF16) for i in range(2)]

        sv, ss, sg, sm = (p.sem(pre + n) for n in ("sv", "ss", "sg", "sm"))
        l_w = p.sems(pre + "l_w", 2)
        l_q = p.sems(pre + "l_q", 2)
        l_kt = p.sems(pre + "l_kt", 2)
        l_mk = p.sem(pre + "l_mk")
        l_vt = p.sems(pre + "l_vt", 2)
        l_g2 = p.sem(pre + "l_g2")
        l_x = p.sems(pre + "l_x", 2)
        s_o = p.sems(pre + "s_o", 1)
        emit_ident(p, ident_bf, sg)
        g_id = emit_ident(p, ident_f, sg)
        v_one = p.op("vector", "memset", ones_bf[:], 1.0, inc=[(sv, 1)])

        wc_v = []
        for f in range(NF):
            w = [(sv, wc_v[-1])] if wc_v else []
            wl = p.dma("sync", l_w[0],
                       [(wst[0][:], wo[128 * f:128 * (f + 1), :])], wait=w)
            cv = p.op("vector", "tensor_copy", wo_bf[:, f, :], wst[0][:],
                      wait=[wl], inc=[(sv, 1)])
            wc_v.append(cv)
        wo_ready = wc_v[-1]

        stv = [0, 0]
        q_ld = [None] * 5
        qk_fin = [0] * 4

        def load_q(j, wq_):
            return p.dma("sync", l_q[j % 2],
                         [(qiq[j % 2][:, f, :],
                           qt2[128 * f:128 * (f + 1), 512 * j:512 * (j + 1)])
                          for f in range(NF)], wait=wq_)

        q_ld[0] = load_q(0, [])
        for iq in range(4):
            w_q = q_ld[iq]
            exp_v = [0] * KT
            msk_v = [0] * KT
            qk_v = [0] * KT
            blk_rd = [0] * 9
            blk_ld = [None] * 9
            den_last = 0

            def load_blk(b, wk_):
                return p.dma("sync", l_kt[b % 2],
                             [(ktb[b % 2][:, f, :], _ktg_slice(ktg, f, b))
                              for f in range(NF)], wait=wk_)

            blk_ld[0] = load_blk(0, [(sm, sm.n)] if iq > 0 else [])

            def emit_qk1(kt, extra):
                par = kt % 2
                ktl = kt % 8
                for f in range(NF):
                    w = list(extra) if f == 0 else []
                    qk_v[kt] = p.op("tensor", "matmul", ps_sc[par][:],
                                    ktb[(kt // 8) % 2][:, f,
                                                       128 * ktl:128 * (ktl + 1)],
                                    qiq[iq % 2][:, f, :], start=(f == 0),
                                    stop=(f == NF - 1), wait=w,
                                    inc=([(sm, 1)] if f == NF - 1 else []))

            w0 = [w_q, blk_ld[0]]
            if iq > 0:
                w0.append((sv, sv.n))
            emit_qk1(0, w0)
            for ktblk in range(8):
                if ktblk + 1 < 8:
                    blk_ld[ktblk + 1] = load_blk(
                        ktblk + 1,
                        [(sm, blk_rd[ktblk - 1])] if ktblk >= 1 else [])
                wm_ = []
                if ktblk > 0:
                    wm_ = [(sv, msk_v[8 * ktblk - 1])]
                elif iq > 0:
                    wm_ = [(sv, sv.n)]
                w_mk = p.dma("gpsimd", l_mk,
                             [(mkb[:, ii, :, :].rearrange("p t q -> p (t q)"),
                               mask[4 * iq + ii, ktblk].rearrange("p t q -> p (t q)"))
                              for ii in range(4)], wait=wm_)
                for ktl in range(8):
                    kt = 8 * ktblk + ktl
                    par = kt % 2
                    we = [(sm, qk_v[kt])]
                    if kt >= 1:
                        we.append((sv, msk_v[kt - 1]))
                    exp_v[kt] = p.op("scalar", "activation", out=pq[:],
                                     in_=ps_sc[par][:],
                                     func=AF.Exp, bias=0.0, scale=L1_SCALE,
                                     wait=we, inc=[(ss, 1)])
                    if kt + 1 < KT:
                        wq2 = [(ss, exp_v[kt - 1])] if kt >= 1 else []
                        if (kt + 1) % 8 == 0:
                            wq2.append(blk_ld[(kt + 1) // 8])
                        emit_qk1(kt + 1, wq2)
                        if (kt + 1) % 8 == 7:
                            blk_rd[(kt + 1) // 8] = qk_v[kt + 1]
                    msk_v[kt] = p.op(
                        "vector", "tensor_mul",
                        PT[:, kt, :].rearrange("p (i q) -> p i q", i=4),
                        pq[:].rearrange("p (i q) -> p i q", i=4),
                        mkb[:, :, ktl, :],
                        wait=[(ss, exp_v[kt]), w_mk], inc=[(sv, 1)])
                    for ii in range(4):
                        w = [(sv, msk_v[kt]), (sv, v_one)] if ii == 0 else []
                        den_last = p.op("tensor", "matmul", ps_o[0][:, ii:ii + 1],
                                        PT[:, kt, 128 * ii:128 * (ii + 1)],
                                        ones_bf[:, 0:1],
                                        start=(kt == 0 and ii == 0),
                                        stop=(kt == KT - 1 and ii == 3),
                                        wait=w,
                                        inc=([(sm, 1)] if ii == 3 else []))
            qk_fin[iq] = qk_v[KT - 1]
            if iq + 1 < 4:
                q_ld[iq + 1] = load_q(iq + 1,
                                      [(sm, qk_fin[iq - 1])] if iq >= 1 else [])
            vds = p.op("vector", "tensor_copy", dsb[:], ps_o[0][:, 0:4],
                       wait=[(sm, den_last)], inc=[(sv, 1)])
            vdr = p.op("vector", "reciprocal", drc[:], dsb[:], wait=[(sv, vds)],
                       inc=[(sv, 1)])
            norm_last = 0
            blkav = [0] * 16
            vld = [None] * 16

            def load_vt(j):
                dvh_, kb_ = divmod(j, 8)
                w = [(sm, blkav[j - 2])] if j >= 2 else []
                return p.dma("gpsimd", l_vt[j % 2],
                             [(vtb[j % 2][:],
                               vg[1024 * kb_:1024 * (kb_ + 1),
                                  512 * dvh_:512 * (dvh_ + 1)]
                               .rearrange("(t p) c -> p t c", p=128))], wait=w)

            vld[0] = load_vt(0)
            vld[1] = load_vt(1)
            for dvh in range(2):
                for ktblk in range(8):
                    j = 8 * dvh + ktblk
                    bb = j % 2
                    av_last = 0
                    for ktl in range(8):
                        kt = 8 * ktblk + ktl
                        for ii in range(4):
                            w = []
                            if ktl == 0 and ii == 0:
                                w = [vld[j]]
                                if ktblk == 0:
                                    w.append((sv, norm_last))
                                    w.append((sv, vds))
                                    if dvh == 0:
                                        w.append((sv, msk_v[KT - 1]))
                            av_last = p.op(
                                "tensor", "matmul", ps_o[ii][:],
                                PT[:, kt, 128 * ii:128 * (ii + 1)],
                                vtb[bb][:, ktl, :],
                                start=(kt == 0), stop=(kt == KT - 1), wait=w,
                                inc=([(sm, 1)] if (ktl == 7 and ii == 3) else []))
                    blkav[j] = av_last
                    if j + 2 < 16:
                        vld[j + 2] = load_vt(j + 2)
                for ii in range(4):
                    w = [(sm, blkav[8 * dvh + 7]), (sv, vdr)] if ii == 0 else []
                    norm_last = p.op("vector", "tensor_scalar_mul",
                                     o_bf3[:, ii, 512 * dvh:512 * (dvh + 1)],
                                     ps_o[ii][:], drc[:, ii:ii + 1], wait=w,
                                     inc=[(sv, 1)])
            o_done = norm_last
            oT_done = 0
            for ii in range(4):
                wf = [(sg, g_id), (sv, o_done)]
                if oT_done:
                    wf.append((sv, oT_done))
                oT_done = emit_transpose8(p, oT, 128 * ii, o_bf3[:, ii, :],
                                          ident_bf, ps_tb, sm, sv, wait_first=wf)
            w_g2 = p.dma("sync", l_g2,
                         [(g2c[:, f, :],
                           g2t[128 * f:128 * (f + 1), 512 * iq:512 * (iq + 1)])
                          for f in range(NF)],
                         wait=([(sv, sv.n)] if iq > 0 else []))
            u_done = 0
            pr_prev = [0, 0]
            for m in range(NF):
                for f in range(NF):
                    w = []
                    if f == 0:
                        w = [(sv, max(oT_done, wo_ready))]
                        if pr_prev[m % 2]:
                            w.append((sv, pr_prev[m % 2]))
                    tp = p.op("tensor", "matmul", ps_sc[m % 2][:],
                              wo_bf[:, f, 128 * m:128 * (m + 1)], oT[:, f, :],
                              start=(f == 0), stop=(f == NF - 1), wait=w,
                              inc=([(sm, 1)] if f == NF - 1 else []))
                vs2 = p.op("vector", "tensor_copy", s2c[:], ps_sc[m % 2][:],
                           wait=[(sm, tp)] + ([(sv, u_done)] if u_done else []),
                           inc=[(sv, 1)])
                u_done = p.op("vector", "tensor_mul", uT[:, m, :], s2c[:],
                              g2c[:, m, :],
                              wait=[(sv, vs2), w_g2] +
                                   ([(sm, sm.n)] if (iq > 0 and m == 0) else []),
                              inc=[(sv, 1)])
                pr_prev[m % 2] = vs2
            res_v = 0
            tr_done = 0
            for tb4 in range(4):
                ti = 4 * iq + tb4
                wl = p.dma("sync", l_x[0],
                           [(x_t[0][:], x1_i[128 * ti:128 * (ti + 1), :])],
                           wait=([(sv, res_v)] if ti >= 1 else []))
                for half in range(2):
                    pf = ps_o[half]
                    for j in range(4):
                        m = 4 * half + j
                        w = []
                        if m == 0 and tb4 == 0:
                            w = [(sv, u_done), (sg, g_id)]
                        if j == 0 and tr_done:
                            w.append((sv, tr_done))
                        p.op("tensor", "transpose",
                             pf[:, 128 * j:128 * (j + 1)],
                             uT[:, m, 128 * tb4:128 * (tb4 + 1)], ident_f[:],
                             wait=w, inc=([(sm, 1)] if j == 3 else []))
                    wv_ = [(sm, sm.n), wl]
                    if half == 0 and ti >= 1:
                        wv_.append((s_o[0], stv[0]))
                    tr_done = res_v = p.op(
                        "vector", "tensor_add",
                        xo_t[0][:, 512 * half:512 * (half + 1)],
                        x_t[0][:, 512 * half:512 * (half + 1)],
                        pf[:, 0:512], wait=wv_, inc=[(sv, 1)])
                p.wait("gpsimd", sv, res_v)
                _, stv[0] = p.dma(
                    "gpsimd", s_o[0],
                    [(out_o[128 * ti:128 * (ti + 1), :], xo_t[0][:])])
        p.barrier("gpsimd", s_o[0])


def _mk_io(nc, specs, kind):
    return {name: nc.dram_tensor(name, list(shape), dt, kind=kind).ap()
            for name, (shape, dt) in specs.items()}


def _standalone(emit, ins, outs):
    nc = bass.Bass("TRN2", target_bir_lowering=False)
    io = _mk_io(nc, ins, "ExternalInput")
    io.update(_mk_io(nc, outs, "ExternalOutput"))
    with contextlib.ExitStack() as ctx:
        p = Prog(nc, ctx)
        emit(nc, p, ctx, io)
        with nc.Block() as block:
            p.build(block)
    return nc


def build_l0pre():
    return _standalone(emit_l0pre, L0PRE_IO, L0PRE_OUT)


def build_l0attn():
    return _standalone(emit_l0attn, L0ATTN_IO, L0ATTN_OUT)


def build_l0rwkv():
    return _standalone(emit_l0rwkv, L0RWKV_IO, L0RWKV_OUT)


def build_l1pre():
    return _standalone(emit_l1pre, L1PRE_IO, L1PRE_OUT)


def build_l1attn():
    return _standalone(emit_l1attn, L1ATTN_IO, L1ATTN_OUT)


def build_mega():
    nc = bass.Bass("TRN2", target_bir_lowering=False)
    ext = {}
    for specs in (L0PRE_IO, L0RWKV_IO, L1PRE_IO, L1ATTN_IO, L0ATTN_IO):
        for name, (shape, dt) in specs.items():
            if name in ext or name in ("xnt", "st", "x1", "qt", "kt_g", "v_g",
                                       "qt2", "kt2_g", "v2_g", "g2t"):
                continue
            ext[name] = nc.dram_tensor(name, list(shape), dt,
                                       kind="ExternalInput").ap()
    itn = {}
    for name, (shape, dt) in dict(
            xnt=([D, TOK + 1], BF16), qt=([D, TOK], BF16), kt=([D, TOK], BF16),
            v=([TOK, D], BF16), kt_g=([4 * D, TOK], BF16),
            v_g=([4 * TOK, D], BF16), st=([D, TOK], F32), x1=([TOK, D], F32),
            qt2=([D, TOK], BF16), kt2=([D, TOK], BF16), v2=([TOK, D], BF16),
            g2t=([D, TOK], BF16), kt2_g=([4 * D, TOK], BF16),
            v2_g=([4 * TOK, D], BF16)).items():
        itn[name] = nc.dram_tensor(name, list(shape), dt, kind="Internal").ap()
    out = nc.dram_tensor("out", [TOK, D], F32, kind="ExternalOutput").ap()
    io = dict(ext)
    io.update(itn)
    io["out"] = out
    with contextlib.ExitStack() as octx:
        p = Prog(nc, octx)

        def hook0(pp, waits):
            cc = pp.sem("b_cc")
            for in_ap, out_ap in ((itn["kt"], itn["kt_g"]), (itn["v"], itn["v_g"])):
                pp.op("gpsimd", "collective_compute", "AllGather", ALU.bypass,
                      replica_groups=GROUPS, ins=[in_ap.opt()],
                      outs=[out_ap.opt()], wait=waits, inc=[(cc, 1)])
                waits = []

        def hook1(pp, waits):
            cc = pp.sem("f_cc")
            for in_ap, out_ap in ((itn["kt2"], itn["kt2_g"]),
                                  (itn["v2"], itn["v2_g"])):
                pp.op("gpsimd", "collective_compute", "AllGather", ALU.bypass,
                      replica_groups=GROUPS, ins=[in_ap.opt()],
                      outs=[out_ap.opt()], wait=waits, inc=[(cc, 1)])
                waits = []

        with contextlib.ExitStack() as c1:
            emit_l0pre(nc, p, c1, io, pre="a_", hook=hook0)
            p.phase_barrier()
        with contextlib.ExitStack() as c3:
            emit_l0attn(nc, p, c3, io, pre="c_")
            p.phase_barrier()
        with contextlib.ExitStack() as c4:
            emit_l0rwkv(nc, p, c4, io, pre="d_")
            p.phase_barrier()
        with contextlib.ExitStack() as c5:
            emit_l1pre(nc, p, c5, io, pre="e_", hook=hook1)
            p.phase_barrier()
        with contextlib.ExitStack() as c7:
            emit_l1attn(nc, p, c7, io, pre="g_")
        with nc.Block() as block:
            p.build(block)
    return nc


_BUILDERS = {}


def _get(name, fn):
    if name not in _BUILDERS:
        _BUILDERS[name] = fn()
    return _BUILDERS[name]


def np_split_inputs(inputs):
    x = np.ascontiguousarray(inputs["L0_attn_x"])
    half = 32
    freqs = (1.0 / 500.0 ** (np.arange(half, dtype=np.float64) / half))
    maps = []
    for c in range(8):
        b, r = divmod(c, 4)
        t0 = r * TOK
        xw = np.zeros((TOK + 1, D), np.float32)
        xw[1:] = x[b, t0:t0 + TOK]
        if r > 0:
            xw[0] = x[b, t0 - 1]
        pos = np.arange(t0, t0 + TOK, dtype=np.float64)
        ang = pos[:, None] * freqs[None, :]
        cos_t = np.ascontiguousarray(np.tile(np.cos(ang), (1, H)).astype(np.float32))
        sin_t = np.ascontiguousarray(np.tile(np.sin(ang), (1, H)).astype(np.float32))
        qg = (t0 + np.arange(TOK))
        kg = np.arange(L)
        mq = (kg[None, :] <= qg[:, None])
        mkk = mq.T.astype(NPBF)
        mkk = mkk.reshape(64, 128, 16, 128).transpose(2, 0, 1, 3)
        mkk = mkk.reshape(16, 8, 8, 128, 128).transpose(0, 1, 3, 2, 4)
        maps.append(dict(
            x_own=xw, rope_cos=cos_t, rope_sin=sin_t,
            mask=np.ascontiguousarray(mkk),
            w_qkv=inputs["L0_attn_w_qkv"],
            b_qkv=np.ascontiguousarray(inputs["L0_attn_b_qkv"].reshape(1, 3 * D)),
            w_out=inputs["L0_attn_w_out"],
            b_out=np.ascontiguousarray(inputs["L0_attn_b_out"].reshape(NF, 128).T),
            w_r=inputs["L0_ffn_W_r"], w_k=inputs["L0_ffn_W_k"],
            w_v=inputs["L0_ffn_W_v"],
            mu_r=np.ascontiguousarray(inputs["L0_ffn_mu_r"].reshape(NF, 128).T),
            mu_k=np.ascontiguousarray(inputs["L0_ffn_mu_k"].reshape(NF, 128).T),
            wq1=inputs["L1_attn_Wq"], wk1=inputs["L1_attn_Wk"],
            wv1=inputs["L1_attn_Wv"], wo1=inputs["L1_attn_Wo"],
            wt1=inputs["L1_ffn_weight"],
            alpha1=np.ascontiguousarray(inputs["L1_ffn_alpha"].reshape(1, 1)),
            bias1=np.ascontiguousarray(inputs["L1_ffn_bias"].reshape(NF, 128).T),
        ))
    return maps


def kernel(**inputs):
    maps = np_split_inputs(inputs)
    core_ids = list(range(8))
    nc = _get("mega", build_mega)
    in_maps = [dict(x_own=m["x_own"], w_qkv=m["w_qkv"], b_qkv=m["b_qkv"],
                    rope_cos=m["rope_cos"], rope_sin=m["rope_sin"],
                    w_out=m["w_out"], b_out=m["b_out"], w_r=m["w_r"],
                    w_k=m["w_k"], w_v=m["w_v"], mu_r=m["mu_r"], mu_k=m["mu_k"],
                    wq1=m["wq1"], wk1=m["wk1"], wv1=m["wv1"], wt1=m["wt1"],
                    alpha1=m["alpha1"], bias1=m["bias1"], wo1=m["wo1"],
                    mask=m["mask"]) for m in maps]
    res = run_bass_kernel_spmd(nc, in_maps, core_ids=core_ids)
    out = np.empty((B, L, D), np.float32)
    for c in core_ids:
        b, r = divmod(c, 4)
        out[b, r * TOK:(r + 1) * TOK, :] = res.results[c]["out"]
    return out
